# version 02 (final)
# baseline (speedup 1.0000x reference)
import numpy as np
import ml_dtypes
from contextlib import ExitStack
import concourse.bass as bass
import concourse.mybir as mybir
from concourse.bass_utils import run_bass_kernel_spmd

F32 = mybir.dt.float32
BF16 = mybir.dt.bfloat16
AF = mybir.ActivationFunctionType
ALU = mybir.AluOpType
AX = mybir.AxisListType
P = 128
NEG = -1.0e30
BF = ml_dtypes.bfloat16

B_FULL, T_FULL, D = 32, 2048, 1024
NCORE = 8
CC = 128
LCH = 128
DFF, NEXP = 4096, 8


class Em:
    def __init__(self, nc, names):
        self._cm = []
        self.s = {}
        self.c = {}
        for n in names:
            cm = nc.semaphore(n)
            self.s[n] = cm.__enter__()
            self._cm.append(cm)
            self.c[n] = 0

    def inc(self, inst, name, n=1):
        self.c[name] += n
        inst.then_inc(self.s[name], n)

    def wait(self, eng, name, cnt=None):
        eng.wait_ge(self.s[name], self.c[name] if cnt is None else cnt)

    def close(self):
        for cm in reversed(self._cm):
            cm.__exit__(None, None, None)



def full_barrier(blk, em):
    for name in ("tensor", "vector", "scalar", "gpsimd", "sync"):
        @getattr(blk, name)
        def _(e):
            for snm in ("ld", "st", "tn", "dv", "ac", "gp", "fn", "sx", "cst"):
                em.wait(e, snm)


def build_l0(nseq, T, dbg=False, n_exp_run=8):
    ntok = nseq * T
    NCH = T // LCH
    NINST = nseq * NCH
    NT128 = ntok // P
    NG = ntok // 512
    NT512 = ntok // 512
    assert ntok % 512 == 0 and NINST <= 512
    nc = bass.Bass("TRN2", target_bir_lowering=False)
    dt = nc.dram_tensor
    ap = {}
    for n, sh in [("x", [ntok, D]),
                  ("wq", [D, CC]), ("wk", [D, CC]), ("wv", [D, CC]), ("wga", [D, CC]),
                  ("wa", [D, 1]), ("wb", [D, 1]), ("wo", [CC, D]),
                  ("bq_c", [P, 1]), ("bk_b", [P, CC]), ("bv_b", [P, CC]), ("bg_b", [P, CC]),
                  ("ab_c", [1, 2]), ("bo_b", [P, D]),
                  ("w1", [NEXP, P, 8 * DFF]), ("b1c", [NEXP, P, DFF // P]),
                  ("w2", [NEXP, P, 32 * D]), ("b2r", [NEXP, D]),
                  ("wgm", [D, NEXP]), ("bgm_b", [P, NEXP]),
                  ("ident", [P, P]), ("uptri", [P, P]), ("ones_c", [P, 1]),
                  ("m_sl", [P, P]), ("m_us", [P, P]), ("m_ui", [P, P])]:
        ap[n] = dt(n, sh, F32, kind="ExternalInput").ap()
    ap["x1"] = dt("x1", [ntok, D], F32, kind="ExternalOutput").ap()
    ap["gc"] = dt("gc", [nseq, D], F32, kind="ExternalOutput").ap()
    kd = "ExternalOutput" if dbg else "Internal"
    for n, sh, d in [("xnb", [D, ntok], BF16), ("qf", [CC, ntok], F32),
                     ("ktm", [ntok, CC], F32), ("kfm", [CC, ntok], F32),
                     ("kbfm", [CC, ntok], F32), ("vtm", [ntok, CC], F32),
                     ("gtm", [ntok, CC], F32), ("yatt", [ntok, D], F32),
                     ("acc", [ntok, D], F32)]:
        ap[n] = dt(n, sh, d, kind=kd).ap()
    if dbg:
        ap["dbg1"] = dt("dbg1", [P, 4 * NINST + 8 * NT128], F32, kind="ExternalOutput").ap()
        ap["dbg2"] = dt("dbg2", [P, 8 * NT128], F32, kind="ExternalOutput").ap()
        ap["dbg3"] = dt("dbg3", [P, 768], F32, kind="ExternalOutput").ap()

    es = ExitStack()
    sb = lambda n, sh, d=F32: es.enter_context(nc.sbuf_tensor(n, sh, d))
    WCH = 4 * CC + 2
    s_w = sb("s_w", [P, 8 * WCH], BF16)
    s_wo = sb("s_wo", [P, D], BF16)
    s_wgm = sb("s_wgm", [P, 8 * NEXP], BF16)
    s_b2 = sb("s_b2", [NEXP, D], BF16)
    s_id = sb("s_id", [P, P], F32)
    s_idb = sb("s_idb", [P, P], BF16)
    s_ut = sb("s_ut", [P, P], F32)
    s_ones = sb("s_ones", [P, 1], F32)
    s_orow = sb("s_orow", [1, P], F32)
    s_msl = sb("s_msl", [P, P], F32)
    s_mus = sb("s_mus", [P, P], F32)
    s_mui = sb("s_mui", [P, P], F32)
    s_bq = sb("s_bq", [P, 1], F32)
    s_bkb = sb("s_bkb", [P, CC], F32)
    s_bvb = sb("s_bvb", [P, CC], F32)
    s_bgb = sb("s_bgb", [P, CC], F32)
    s_ab = sb("s_ab", [1, 2], F32)
    s_bob = sb("s_bob", [P, D], F32)
    s_bgmb = sb("s_bgmb", [P, NEXP], F32)
    s_cb = sb("s_cb", [P, 4], F32)
    s_lacol = sb("s_lacol", [P, NINST], F32)
    s_bcol = sb("s_bcol", [P, NINST], F32)
    s_comb = sb("s_comb", [P, 8 * NT128], F32)
    s_dbgl = sb("s_dbgl", [P, 8 * NT128], F32)
    s_dbgw = sb("s_dbgw", [P, 768], F32)

    pf = [es.enter_context(nc.psum_tensor(f"pf{i}", [P, 512], F32)) for i in range(7)]
    pA, pB, pC, pD, pE, pF, pG = pf
    pT = es.enter_context(nc.psum_tensor("pT", [P, 1024], BF16))

    em = Em(nc, ["ld", "st", "tn", "dv", "ac", "gp", "fn", "sx", "cst"])

    with es, nc.Block() as blk:
        @blk.sync
        def _(sy):
            for dst, src in [(s_id, "ident"), (s_ut, "uptri"), (s_msl, "m_sl"),
                             (s_mus, "m_us"), (s_mui, "m_ui"), (s_bq, "bq_c"),
                             (s_bkb, "bk_b"), (s_bvb, "bv_b"), (s_bgb, "bg_b"),
                             (s_ab, "ab_c"), (s_bob, "bo_b"), (s_bgmb, "bgm_b"),
                             (s_ones, "ones_c")]:
                em.inc(sy.dma_start(out=dst[:], in_=ap[src]), "ld", 16)

        with nc.sbuf_tensor("pre_st", [P, 2048], F32) as s_stage:
            jobs = []
            for dc in range(8):
                jobs.append(([(s_stage, 0, CC, ap["wq"][dc * P:(dc + 1) * P, :]),
                              (s_stage, CC, CC, ap["wk"][dc * P:(dc + 1) * P, :]),
                              (s_stage, 2 * CC, CC, ap["wv"][dc * P:(dc + 1) * P, :]),
                              (s_stage, 3 * CC, CC, ap["wga"][dc * P:(dc + 1) * P, :]),
                              (s_stage, 4 * CC, 1, ap["wa"][dc * P:(dc + 1) * P, :]),
                              (s_stage, 4 * CC + 1, 1, ap["wb"][dc * P:(dc + 1) * P, :])],
                             s_w[:, dc * WCH:(dc + 1) * WCH], WCH, P))
            jobs.append(([(s_stage, 0, D, ap["wo"])], s_wo[:], D, P))
            wgl = [(s_stage, dc * NEXP, NEXP, ap["wgm"][dc * P:(dc + 1) * P, :]) for dc in range(8)]
            jobs.append((wgl, s_wgm[:], 8 * NEXP, P))
            jobs.append(([(s_stage, 0, D, ap["b2r"])], s_b2[:], D, NEXP))
            half = [0, 1024]
            for j, (loads, dst, width, rows) in enumerate(jobs):
                off = half[j % 2]

                @blk.sync
                def _(sy, loads=loads, off=off, j=j, rows=rows):
                    if j >= 2:
                        em.wait(sy, "cst", j - 1)
                    for (st_t, c0, w, src) in loads:
                        em.inc(sy.dma_start(out=st_t[0:rows, off + c0:off + c0 + w], in_=src), "ld", 16)

                @blk.vector
                def _(v, dst=dst, off=off, width=width, j=j, rows=rows):
                    em.wait(v, "ld")
                    em.inc(v.tensor_copy(dst, s_stage[0:rows, off:off + width]), "cst", 1)

            @blk.vector
            def _(v):
                em.wait(v, "cst")
                em.wait(v, "ld")
                if dbg:
                    v.tensor_copy(s_dbgw[:, 64:128], s_wgm[:])
                    v.tensor_copy(s_dbgw[:, 216:224], s_bgmb[:])
                v.tensor_copy(s_idb[:], s_id[:])
                v.memset(s_orow[:], 1.0)
                for i, val in enumerate([1e-5, 1.0, 0.0, 2.0]):
                    inst = v.memset(s_cb[:, i:i + 1], val)
                em.inc(inst, "dv", 1)

            full_barrier(blk, em)
            ph1 = ExitStack()
            _s1 = lambda n, sh, d=F32: ph1.enter_context(nc.sbuf_tensor(n, sh, d))
            sx0 = _s1("a1x0", [P, D]); sx1 = _s1("a1x1", [P, D])
            sx2 = _s1("a1x2", [P, D]); sx3 = _s1("a1x3", [P, D])
            s_mv = _s1("a1mv", [P, 12]); s_ms = _s1("a1ms", [P, 20])
            s_xn = _s1("a1xn", [P, 4 * D], BF16)
            s_xnf = _s1("a1xnf", [P, 8 * 512], BF16)
            s_kt = _s1("a1kt", [P, CC]); s_kb = _s1("a1kb", [P, CC])
            s_vg = _s1("a1vg", [P, 2 * CC])
            s_col = _s1("a1col", [P, 8])
            s_q5 = _s1("a1q5", [P, 512])
            s_row = _s1("a1row", [1, 2048])
            s_ktf = _s1("a1ktf", [P, 2 * CC])
            if True:
                s_x = [sx0, sx1, sx2, sx3]
                for g in range(NG):
                    @blk.sync
                    def _(sy, g=g):
                        if g >= 1:
                            sy.wait_ge(em.s["fn"], 16 * g)
                        for s4 in range(4):
                            ti = 4 * g + s4
                            em.inc(sy.dma_start(out=s_x[s4][:], in_=ap["x"][ti * P:(ti + 1) * P, :]), "ld", 16)
                    ld_h = em.c["ld"]

                    @blk.vector
                    def _(v, ld_h=ld_h):
                        em.wait(v, "dv")
                        v.wait_ge(em.s["ld"], ld_h)
                        for s4 in range(4):
                            xt = s_x[s4]
                            v.bn_stats(s_mv[:, 0:6], xt[:, 0:512])
                            inst = v.bn_stats(s_mv[:, 6:12], xt[:, 512:1024])
                            em.inc(inst, "dv", 1)
                            em.wait(v, "dv")
                            inst = v.bn_aggr(s_ms[:, 4 * s4:4 * s4 + 2], s_mv[:, 0:12])
                            em.inc(inst, "dv", 1)
                            em.wait(v, "dv")
                            inst = v.tensor_scalar(out=s_ms[:, 4 * s4 + 2:4 * s4 + 3],
                                                   in0=s_ms[:, 4 * s4:4 * s4 + 1],
                                                   scalar1=-1.0, scalar2=None, op0=ALU.mult)
                        em.inc(inst, "dv", 1)

                    @blk.scalar
                    def _(s):
                        em.wait(s, "ac")
                        em.wait(s, "dv")
                        for s4 in range(4):
                            s.activation(out=s_ms[:, 4 * s4 + 3:4 * s4 + 4],
                                         in_=s_ms[:, 4 * s4 + 1:4 * s4 + 2], func=AF.Ln,
                                         bias=s_cb[:, 0:1])
                            inst = s.activation(out=s_ms[:, 4 * s4 + 3:4 * s4 + 4],
                                                in_=s_ms[:, 4 * s4 + 3:4 * s4 + 4], func=AF.Exp, scale=-0.5)
                        em.inc(inst, "ac", 1)

                    @blk.vector
                    def _(v):
                        em.wait(v, "dv")
                        em.wait(v, "ac")
                        for s4 in range(4):
                            inst = v.tensor_scalar(out=s_xn[:, s4 * D:(s4 + 1) * D], in0=s_x[s4][:],
                                                   scalar1=s_ms[:, 4 * s4 + 2:4 * s4 + 3], op0=ALU.add,
                                                   scalar2=s_ms[:, 4 * s4 + 3:4 * s4 + 4], op1=ALU.mult)
                        em.inc(inst, "dv", 1)

                    for s4 in range(4):
                        @blk.tensor
                        def _(t, s4=s4):
                            em.wait(t, "dv")
                            em.wait(t, "sx")
                            for c8 in range(8):
                                inst = t.transpose(pT[:, c8 * P:(c8 + 1) * P],
                                                   s_xn[:, s4 * D + c8 * P:s4 * D + (c8 + 1) * P],
                                                   identity=s_idb[:])
                            em.inc(t.drain(), "tn", 1)

                        @blk.vector
                        def _(v, s4=s4):
                            em.wait(v, "tn")
                            for c8 in range(8):
                                inst = v.tensor_copy(s_xnf[:, c8 * 512 + s4 * P:c8 * 512 + (s4 + 1) * P],
                                                     pT[:, c8 * P:(c8 + 1) * P])
                            em.inc(inst, "sx", 1)

                    @blk.tensor
                    def _(t):
                        em.wait(t, "dv")
                        em.wait(t, "sx")
                        for dc in range(8):
                            t.matmul(pD[:, 0:512], s_w[:, dc * WCH:dc * WCH + CC],
                                     s_xnf[:, dc * 512:(dc + 1) * 512], start=(dc == 0), stop=(dc == 7))
                        for dc in range(8):
                            t.matmul(pE[0:1, 0:512], s_w[:, dc * WCH + 4 * CC:dc * WCH + 4 * CC + 1],
                                     s_xnf[:, dc * 512:(dc + 1) * 512], start=(dc == 0), stop=(dc == 7))
                        for dc in range(8):
                            inst = t.matmul(pF[0:1, 0:512], s_w[:, dc * WCH + 4 * CC + 1:dc * WCH + WCH],
                                            s_xnf[:, dc * 512:(dc + 1) * 512], start=(dc == 0), stop=(dc == 7))
                        em.inc(t.drain(), "tn", 1)

                    @blk.vector
                    def _(v):
                        em.wait(v, "tn")
                        em.wait(v, "dv")
                        v.tensor_scalar(out=s_q5[:], in0=pD[:, 0:512], scalar1=s_bq[:, 0:1],
                                        scalar2=None, op0=ALU.add)
                        v.tensor_scalar(out=s_row[0:1, 0:512], in0=pE[0:1, 0:512],
                                        scalar1=s_ab[0:1, 0:1], scalar2=None, op0=ALU.add)
                        inst = v.tensor_scalar(out=s_row[0:1, 512:1024], in0=pF[0:1, 0:512],
                                                scalar1=s_ab[0:1, 1:2], scalar2=None, op0=ALU.add)
                        em.inc(inst, "dv", 1)
                        em.wait(v, "dv")
                        inst = v.scalar_tensor_tensor(out=s_row[0:1, 1024:1536], in0=s_row[0:1, 0:512],
                                                      scalar=-1.0, in1=s_row[0:1, 0:512],
                                                      op0=ALU.mult, op1=ALU.max)
                        em.inc(inst, "dv", 1)

                    @blk.scalar
                    def _(s):
                        em.wait(s, "dv")
                        em.wait(s, "ac")
                        s.activation(out=s_row[0:1, 512:1024], in_=s_row[0:1, 512:1024], func=AF.Sigmoid)
                        s.activation(out=s_row[0:1, 1536:2048], in_=s_row[0:1, 1024:1536], func=AF.Exp, scale=-1.0)
                        inst = s.activation(out=s_row[0:1, 1536:2048], in_=s_row[0:1, 1536:2048],
                                            func=AF.Ln, bias=s_cb[0:1, 1:2])
                        em.inc(inst, "ac", 1)

                    @blk.vector
                    def _(v):
                        em.wait(v, "ac")
                        em.wait(v, "dv")
                        inst = v.tensor_scalar(out=s_row[0:1, 1024:1536], in0=s_row[0:1, 0:512],
                                                scalar1=0.0, scalar2=None, op0=ALU.min)
                        em.inc(inst, "dv", 1)
                        em.wait(v, "dv")
                        inst = v.tensor_tensor(out=s_row[0:1, 0:512], in0=s_row[0:1, 1024:1536],
                                               in1=s_row[0:1, 1536:2048], op=ALU.subtract)
                        em.inc(inst, "dv", 1)

                    @blk.tensor
                    def _(t):
                        em.wait(t, "dv")
                        for s4 in range(4):
                            t.transpose(pG[:, s4:s4 + 1], s_row[0:1, s4 * P:(s4 + 1) * P], identity=s_id[0:1, 0:1])
                            inst = t.transpose(pG[:, 4 + s4:5 + s4],
                                               s_row[0:1, 512 + s4 * P:512 + (s4 + 1) * P], identity=s_id[0:1, 0:1])
                        em.inc(t.drain(), "tn", 1)

                    @blk.vector
                    def _(v, g=g):
                        em.wait(v, "tn")
                        v.tensor_copy(s_lacol[:, 4 * g:4 * g + 4], pG[:, 0:4])
                        inst = v.tensor_copy(s_bcol[:, 4 * g:4 * g + 4], pG[:, 4:8])
                        em.inc(inst, "dv", 1)

                    for s4 in range(4):
                        @blk.tensor
                        def _(t, s4=s4):
                            em.wait(t, "dv")
                            em.wait(t, "sx")
                            for dc in range(8):
                                t.matmul(pA[:, 0:CC], s_xnf[:, dc * 512 + s4 * P:dc * 512 + (s4 + 1) * P],
                                         s_w[:, dc * WCH + CC:dc * WCH + 2 * CC], start=(dc == 0), stop=(dc == 7))
                            for dc in range(8):
                                t.matmul(pB[:, 0:CC], s_xnf[:, dc * 512 + s4 * P:dc * 512 + (s4 + 1) * P],
                                         s_w[:, dc * WCH + 2 * CC:dc * WCH + 3 * CC], start=(dc == 0), stop=(dc == 7))
                            for dc in range(8):
                                inst = t.matmul(pC[:, 0:CC], s_xnf[:, dc * 512 + s4 * P:dc * 512 + (s4 + 1) * P],
                                                s_w[:, dc * WCH + 3 * CC:dc * WCH + 4 * CC], start=(dc == 0), stop=(dc == 7))
                            em.inc(t.drain(), "tn", 1)

                        @blk.vector
                        def _(v):
                            em.wait(v, "tn")
                            em.wait(v, "dv")
                            em.wait(v, "st")
                            inst = v.tensor_tensor(out=s_kt[:], in0=pA[:, 0:CC], in1=s_bkb[:], op=ALU.add)
                            em.inc(inst, "dv", 1)

                        @blk.scalar
                        def _(s):
                            em.wait(s, "dv")
                            em.wait(s, "ac")
                            s.activation(out=s_kb[:], in_=s_kt[:], func=AF.Square,
                                         accum_out=s_col[:, 0:1])
                            inst = s.activation(out=s_col[:, 6:7], in_=s_cb[:, 1:2], func=AF.Identity)
                            em.inc(inst, "ac", 1)

                        @blk.vector
                        def _(v):
                            em.wait(v, "ac")
                            em.wait(v, "dv")
                            inst = v.tensor_scalar(out=s_col[:, 2:3], in0=s_col[:, 0:1], scalar1=1e-20,
                                                   scalar2=None, op0=ALU.max)
                            em.inc(inst, "dv", 1)

                        @blk.scalar
                        def _(s):
                            em.wait(s, "dv")
                            em.wait(s, "ac")
                            s.activation(out=s_col[:, 3:4], in_=s_col[:, 2:3], func=AF.Ln)
                            inst = s.activation(out=s_col[:, 3:4], in_=s_col[:, 3:4], func=AF.Exp, scale=-0.5)
                            em.inc(inst, "ac", 1)

                        @blk.vector
                        def _(v, g=g, s4=s4):
                            em.wait(v, "ac")
                            em.wait(v, "dv")
                            inst = v.tensor_scalar(out=s_kt[:], in0=s_kt[:], scalar1=s_col[:, 3:4],
                                                   scalar2=None, op0=ALU.mult)
                            em.inc(inst, "dv", 1)
                            em.wait(v, "dv")
                            v.tensor_scalar(out=s_kb[:], in0=s_kt[:],
                                            scalar1=s_bcol[:, 4 * g + s4:4 * g + s4 + 1],
                                            scalar2=None, op0=ALU.mult)
                            v.tensor_tensor(out=s_vg[:, 0:CC], in0=pB[:, 0:CC], in1=s_bvb[:], op=ALU.add)
                            inst = v.tensor_tensor(out=s_vg[:, CC:2 * CC], in0=pC[:, 0:CC], in1=s_bgb[:], op=ALU.add)
                            em.inc(inst, "dv", 1)

                        @blk.scalar
                        def _(s):
                            em.wait(s, "dv")
                            em.wait(s, "ac")
                            inst = s.activation(out=s_vg[:, CC:2 * CC], in_=s_vg[:, CC:2 * CC], func=AF.Sigmoid)
                            em.inc(inst, "ac", 1)

                        @blk.tensor
                        def _(t):
                            em.wait(t, "dv")
                            t.transpose(pA[:, 0:CC], s_kt[:], identity=s_id[:])
                            inst = t.transpose(pB[:, 0:CC], s_kb[:], identity=s_id[:])
                            em.inc(t.drain(), "tn", 1)

                        @blk.vector
                        def _(v):
                            em.wait(v, "tn")
                            em.wait(v, "dv")
                            v.tensor_copy(s_ktf[:, 0:CC], pA[:, 0:CC])
                            inst = v.tensor_copy(s_ktf[:, CC:2 * CC], pB[:, 0:CC])
                            em.inc(inst, "dv", 1)

                        @blk.gpsimd
                        def _(gq, g=g, s4=s4):
                            ti = 4 * g + s4
                            em.wait(gq, "dv")
                            em.wait(gq, "ac")
                            em.inc(gq.dma_start(out=ap["ktm"][ti * P:(ti + 1) * P, :], in_=s_kt[:]), "st", 16)
                            em.inc(gq.dma_start(out=ap["vtm"][ti * P:(ti + 1) * P, :], in_=s_vg[:, 0:CC]), "st", 16)
                            em.inc(gq.dma_start(out=ap["gtm"][ti * P:(ti + 1) * P, :], in_=s_vg[:, CC:2 * CC]), "st", 16)
                            em.inc(gq.dma_start(out=ap["kfm"][:, ti * P:(ti + 1) * P], in_=s_ktf[:, 0:CC]), "st", 16)
                            em.inc(gq.dma_start(out=ap["kbfm"][:, ti * P:(ti + 1) * P], in_=s_ktf[:, CC:2 * CC]), "st", 16)

                    @blk.gpsimd
                    def _(gq, g=g):
                        em.wait(gq, "dv")
                        em.wait(gq, "sx")
                        em.inc(gq.dma_start(
                            out=ap["xnb"].rearrange("(c p) t -> p c t", p=P)[:, :, g * 512:(g + 1) * 512],
                            in_=s_xnf.ap().rearrange("p (c t) -> p c t", c=8)), "st", 16)
                        inst = gq.dma_start(out=ap["qf"][:, g * 512:(g + 1) * 512], in_=s_q5[:])
                        em.inc(inst, "fn", 16)

            full_barrier(blk, em)
            ph1.close()

            ph2 = ExitStack()
            _s2 = lambda n, sh, d=F32: ph2.enter_context(nc.sbuf_tensor(n, sh, d))
            s_lam = _s2("a2lam", [P, NINST]); s_laml = _s2("a2laml", [P, NINST])
            s_gam = _s2("a2gam", [P, NINST]); s_ngam = _s2("a2ngam", [P, NINST])
            s_gaml = _s2("a2gaml", [P, NINST]); s_wcol = _s2("a2wc", [P, NINST])
            s_l0 = _s2("a2l0", [1, NINST])
            s_lrow = _s2("a2lr", [1, P]); s_nlrow = _s2("a2nlr", [1, P])
            s_ld2 = _s2("a2ld", [P, 2 * 6 * CC])
            s_t1 = _s2("a2t1", [P, P]); s_t2 = _s2("a2t2", [P, P])
            s_Ds = _s2("a2Ds", [P, P]); s_Dus = _s2("a2Dus", [P, P]); s_Dui = _s2("a2Dui", [P, P])
            s_Pk = _s2("a2Pk", [P, P]); s_Zk = _s2("a2Zk", [P, P]); s_X = _s2("a2X", [P, P])
            s_UT = _s2("a2UT", [P, P]); s_MT = _s2("a2MT", [P, P]); s_qn = _s2("a2qn", [P, P])
            s_kw = _s2("a2kw", [P, P]); s_og = _s2("a2og", [P, P])
            s_Nb = _s2("a2N", [P, 4 * P]); s_NTb = _s2("a2NT", [P, 4 * P])
            s_ogfb = _s2("a2ogf", [P, 4 * 512], BF16)
            s_y = _s2("a2y", [P, D])
            if True:
                s_N = [s_Nb[:, j * P:(j + 1) * P] for j in range(nseq)]
                s_NT = [s_NTb[:, j * P:(j + 1) * P] for j in range(nseq)]
                s_ogf = [s_ogfb[:, j * 512:(j + 1) * 512] for j in range(nseq)]

                @blk.tensor
                def _(t):
                    em.wait(t, "dv")
                    inst = t.matmul(pA[:, 0:NINST], s_ut[:], s_lacol[:], start=True, stop=True)
                    em.inc(t.drain(), "tn", 1)

                @blk.vector
                def _(v):
                    em.wait(v, "tn")
                    em.wait(v, "dv")
                    if dbg:
                        v.tensor_copy(s_dbgw[:, 144:208], s_wgm[:])
                    inst = v.tensor_copy(s_lam[:], pA[:, 0:NINST])
                    em.inc(inst, "dv", 1)

                @blk.gpsimd
                def _(gq):
                    em.wait(gq, "dv")
                    inst = gq.dma_start(out=s_l0[0:1, :], in_=s_lam[P - 1:P, :])
                    em.inc(inst, "gp", 16)

                @blk.tensor
                def _(t):
                    em.wait(t, "gp")
                    inst = t.matmul(pB[:, 0:NINST], s_orow[:], s_l0[0:1, :], start=True, stop=True)
                    em.inc(t.drain(), "tn", 1)

                @blk.vector
                def _(v):
                    em.wait(v, "tn")
                    em.wait(v, "dv")
                    inst = v.tensor_copy(s_laml[:], pB[:, 0:NINST])
                    em.inc(inst, "dv", 1)
                    em.wait(v, "dv")
                    inst = v.tensor_tensor(out=s_wcol[:], in0=s_laml[:], in1=s_lam[:], op=ALU.subtract)
                    em.inc(inst, "dv", 1)

                @blk.scalar
                def _(s):
                    em.wait(s, "dv")
                    em.wait(s, "ac")
                    s.activation(out=s_gam[:], in_=s_lam[:], func=AF.Exp)
                    s.activation(out=s_gaml[:], in_=s_laml[:], func=AF.Exp)
                    inst = s.activation(out=s_wcol[:], in_=s_wcol[:], func=AF.Exp)
                    em.inc(inst, "ac", 1)

                @blk.vector
                def _(v):
                    em.wait(v, "ac")
                    em.wait(v, "dv")
                    v.tensor_scalar(out=s_ngam[:], in0=s_gam[:], scalar1=-1.0, scalar2=None, op0=ALU.mult)
                    v.tensor_tensor(out=s_wcol[:], in0=s_wcol[:], in1=s_bcol[:], op=ALU.mult)
                    v.memset(s_Nb[:], 0.0)
                    inst = v.memset(s_NTb[:], 0.0)
                    em.inc(inst, "dv", 1)

                fn_a2 = em.c["fn"]
                for c in range(NCH):
                    for sq in range(nseq):
                        i = c * nseq + sq
                        iid = sq * NCH + c
                        tok0 = sq * T + c * LCH
                        ofs = (i % 2) * 6 * CC

                        @blk.sync
                        def _(sy, i=i, tok0=tok0, ofs=ofs, fn_a2=fn_a2):
                            if i >= 2:
                                sy.wait_ge(em.s["fn"], fn_a2 + i - 1)
                            em.inc(sy.dma_start(out=s_ld2[:, ofs:ofs + CC], in_=ap["qf"][:, tok0:tok0 + P]), "ld", 16)
                            em.inc(sy.dma_start(out=s_ld2[:, ofs + CC:ofs + 2 * CC], in_=ap["kfm"][:, tok0:tok0 + P]), "ld", 16)
                            em.inc(sy.dma_start(out=s_ld2[:, ofs + 2 * CC:ofs + 3 * CC], in_=ap["kbfm"][:, tok0:tok0 + P]), "ld", 16)
                            em.inc(sy.dma_start(out=s_ld2[:, ofs + 3 * CC:ofs + 4 * CC], in_=ap["ktm"][tok0:tok0 + P, :]), "ld", 16)
                            em.inc(sy.dma_start(out=s_ld2[:, ofs + 4 * CC:ofs + 5 * CC], in_=ap["vtm"][tok0:tok0 + P, :]), "ld", 16)
                            em.inc(sy.dma_start(out=s_ld2[:, ofs + 5 * CC:ofs + 6 * CC], in_=ap["gtm"][tok0:tok0 + P, :]), "ld", 16)
                        ld_i = em.c["ld"]
                        Qf = s_ld2[:, ofs:ofs + CC]
                        Kf = s_ld2[:, ofs + CC:ofs + 2 * CC]
                        Kbf = s_ld2[:, ofs + 2 * CC:ofs + 3 * CC]
                        Kt = s_ld2[:, ofs + 3 * CC:ofs + 4 * CC]
                        Vt = s_ld2[:, ofs + 4 * CC:ofs + 5 * CC]
                        Gt = s_ld2[:, ofs + 5 * CC:ofs + 6 * CC]

                        @blk.tensor
                        def _(t, iid=iid):
                            em.wait(t, "dv")
                            inst = t.transpose(pC[0:1, 0:P], s_lam[:, iid:iid + 1], identity=s_id[:])
                            em.inc(t.drain(), "tn", 1)

                        @blk.vector
                        def _(v):
                            em.wait(v, "tn")
                            em.wait(v, "dv")
                            v.tensor_copy(s_lrow[:], pC[0:1, 0:P])
                            inst = v.tensor_scalar(out=s_nlrow[:], in0=pC[0:1, 0:P], scalar1=-1.0,
                                                   scalar2=None, op0=ALU.mult)
                            em.inc(inst, "dv", 1)

                        @blk.tensor
                        def _(t, ld_i=ld_i, sq=sq):
                            em.wait(t, "dv")
                            t.wait_ge(em.s["ld"], ld_i)
                            t.matmul(pA[:, 0:P], s_lrow[:], s_orow[:], start=True, stop=False)
                            t.matmul(pA[:, 0:P], s_orow[:], s_nlrow[:], start=False, stop=True)
                            t.matmul(pB[:, 0:P], s_nlrow[:], s_orow[:], start=True, stop=False)
                            t.matmul(pB[:, 0:P], s_orow[:], s_lrow[:], start=False, stop=True)
                            t.matmul(pC[:, 128:256], Kf, s_N[sq], start=True, stop=True)
                            inst = t.matmul(pC[:, 256:384], Qf, s_NT[sq], start=True, stop=True)
                            em.inc(t.drain(), "tn", 1)

                        @blk.vector
                        def _(v, iid=iid):
                            em.wait(v, "tn")
                            em.wait(v, "dv")
                            v.tensor_tensor(out=s_t1[:], in0=pA[:, 0:P], in1=s_msl[:], op=ALU.add)
                            v.tensor_tensor(out=s_t2[:], in0=pB[:, 0:P], in1=s_mus[:], op=ALU.add)
                            inst = v.tensor_tensor(out=s_Dui[:], in0=pB[:, 0:P], in1=s_mui[:], op=ALU.add)
                            em.inc(inst, "dv", 1)
                            v.scalar_tensor_tensor(out=s_X[:], in0=pC[:, 128:256],
                                                   scalar=s_ngam[:, iid:iid + 1],
                                                   in1=Vt, op0=ALU.mult, op1=ALU.add)
                            inst = v.tensor_scalar(out=s_qn[:], in0=pC[:, 256:384],
                                                   scalar1=s_gam[:, iid:iid + 1],
                                                   scalar2=None, op0=ALU.mult)
                            em.inc(inst, "dv", 1)

                        @blk.scalar
                        def _(s):
                            em.wait(s, "ac")
                            s.wait_ge(em.s["dv"], em.c["dv"] - 1)
                            s.activation(out=s_Ds[:], in_=s_t1[:], func=AF.Exp)
                            s.activation(out=s_Dus[:], in_=s_t2[:], func=AF.Exp)
                            inst = s.activation(out=s_Dui[:], in_=s_Dui[:], func=AF.Exp)
                            em.inc(inst, "ac", 1)

                        @blk.tensor
                        def _(t):
                            em.wait(t, "ac")
                            em.wait(t, "dv")
                            t.matmul(pD[:, 0:P], Kf, Kbf, start=True, stop=True)
                            inst = t.matmul(pD[:, 128:256], Kbf, Kf, start=True, stop=True)
                            em.inc(t.drain(), "tn", 1)

                        @blk.vector
                        def _(v):
                            em.wait(v, "tn")
                            em.wait(v, "dv")
                            v.scalar_tensor_tensor(out=s_Pk[:], in0=pD[:, 0:P], scalar=-1.0,
                                                   in1=s_Ds[:], op0=ALU.mult, op1=ALU.mult)
                            inst = v.scalar_tensor_tensor(out=s_Zk[:], in0=pD[:, 128:256], scalar=-1.0,
                                                          in1=s_Dus[:], op0=ALU.mult, op1=ALU.mult)
                            em.inc(inst, "dv", 1)

                        for lev in range(7):
                            @blk.tensor
                            def _(t, lev=lev):
                                em.wait(t, "dv")
                                reg = (lev % 2) * 128
                                inst = t.matmul(pE[:, reg:reg + P], s_Zk[:], s_X[:], start=True, stop=True)
                                if lev < 6:
                                    t.matmul(pF[:, 0:P], s_Zk[:], s_Pk[:], start=True, stop=True)
                                    inst = t.matmul(pF[:, 128:256], s_Pk[:], s_Zk[:], start=True, stop=True)
                                em.inc(t.drain(), "tn", 1)

                            @blk.vector
                            def _(v, lev=lev):
                                em.wait(v, "tn")
                                em.wait(v, "dv")
                                reg = (lev % 2) * 128
                                inst = v.tensor_tensor(out=s_X[:], in0=pE[:, reg:reg + P], in1=s_X[:], op=ALU.add)
                                if lev < 6:
                                    v.tensor_copy(s_Pk[:], pF[:, 0:P])
                                    inst = v.tensor_copy(s_Zk[:], pF[:, 128:256])
                                em.inc(inst, "dv", 1)

                        @blk.tensor
                        def _(t):
                            em.wait(t, "dv")
                            inst = t.transpose(pG[:, 0:P], s_X[:], identity=s_id[:])
                            em.inc(t.drain(), "tn", 1)

                        @blk.vector
                        def _(v):
                            em.wait(v, "tn")
                            em.wait(v, "dv")
                            inst = v.tensor_copy(s_UT[:], pG[:, 0:P])
                            em.inc(inst, "dv", 1)

                        @blk.tensor
                        def _(t):
                            em.wait(t, "dv")
                            inst = t.matmul(pG[:, 128:256], s_UT[:], Qf, start=True, stop=True)
                            em.inc(t.drain(), "tn", 1)

                        @blk.vector
                        def _(v, iid=iid):
                            em.wait(v, "tn")
                            em.wait(v, "dv")
                            v.scalar_tensor_tensor(out=s_MT[:], in0=pG[:, 128:256],
                                                   scalar=s_bcol[:, iid:iid + 1],
                                                   in1=s_Dui[:], op0=ALU.mult, op1=ALU.mult)
                            inst = v.tensor_scalar(out=s_kw[:], in0=Kt, scalar1=s_wcol[:, iid:iid + 1],
                                                   scalar2=None, op0=ALU.mult)
                            em.inc(inst, "dv", 1)

                        @blk.tensor
                        def _(t):
                            em.wait(t, "dv")
                            t.matmul(pG[:, 256:384], s_MT[:], Kt, start=True, stop=True)
                            t.matmul(pE[:, 256:384], s_kw[:], s_X[:], start=True, stop=True)
                            inst = t.matmul(pE[:, 384:512], s_X[:], s_kw[:], start=True, stop=True)
                            em.inc(t.drain(), "tn", 1)

                        @blk.vector
                        def _(v, sq=sq, iid=iid):
                            em.wait(v, "tn")
                            em.wait(v, "dv")
                            v.tensor_tensor(out=s_og[:], in0=pG[:, 256:384], in1=s_qn[:], op=ALU.add)
                            v.tensor_tensor(out=s_og[:], in0=s_og[:], in1=Gt, op=ALU.mult)
                            v.scalar_tensor_tensor(out=s_N[sq], in0=s_N[sq],
                                                   scalar=s_gaml[:, iid:iid + 1],
                                                   in1=pE[:, 256:384], op0=ALU.mult, op1=ALU.add)
                            inst = v.scalar_tensor_tensor(out=s_NT[sq], in0=s_NT[sq],
                                                          scalar=s_gaml[:, iid:iid + 1],
                                                          in1=pE[:, 384:512], op0=ALU.mult, op1=ALU.add)
                            em.inc(inst, "fn", 1)
                            inst = v.tensor_copy(s_lrow[:], s_lrow[:])
                            em.inc(inst, "dv", 1)

                        @blk.tensor
                        def _(t):
                            em.wait(t, "dv")
                            inst = t.transpose(pG[:, 384:512], s_og[:], identity=s_id[:])
                            em.inc(t.drain(), "tn", 1)

                        @blk.vector
                        def _(v, sq=sq, c=c):
                            em.wait(v, "tn")
                            em.wait(v, "dv")
                            inst = v.tensor_copy(s_ogf[sq][:, (c % 4) * P:(c % 4 + 1) * P], pG[:, 384:512])
                            em.inc(inst, "dv", 1)

                        if c % 4 == 3:
                            for tk in range(4):
                                @blk.tensor
                                def _(t, sq=sq, tk=tk):
                                    em.wait(t, "dv")
                                    t.matmul(pA[:, 0:512], s_ogf[sq][:, tk * P:(tk + 1) * P],
                                             s_wo[:, 0:512], start=True, stop=True)
                                    inst = t.matmul(pB[:, 0:512], s_ogf[sq][:, tk * P:(tk + 1) * P],
                                                    s_wo[:, 512:1024], start=True, stop=True)
                                    em.inc(t.drain(), "tn", 1)

                                @blk.vector
                                def _(v):
                                    em.wait(v, "tn")
                                    em.wait(v, "dv")
                                    em.wait(v, "st")
                                    v.tensor_tensor(out=s_y[:, 0:512], in0=pA[:, 0:512],
                                                    in1=s_bob[:, 0:512], op=ALU.add)
                                    inst = v.tensor_tensor(out=s_y[:, 512:1024], in0=pB[:, 0:512],
                                                           in1=s_bob[:, 512:1024], op=ALU.add)
                                    em.inc(inst, "dv", 1)

                                @blk.gpsimd
                                def _(gq, sq=sq, c=c, tk=tk):
                                    em.wait(gq, "dv")
                                    tok = sq * T + (c - 3) * LCH + tk * P
                                    em.inc(gq.dma_start(out=ap["yatt"][tok:tok + P, :], in_=s_y[:]), "st", 16)

            full_barrier(blk, em)
            ph2.close()

            ph3 = ExitStack()
            _s3 = lambda n, sh, d=F32: ph3.enter_context(nc.sbuf_tensor(n, sh, d))
            mxn0 = _s3("m_xn0", [P, 8 * 512], BF16); mxn1 = _s3("m_xn1", [P, 8 * 512], BF16)
            s_w1h = _s3("m_w1h", [P, 8 * 2048], BF16)
            s_w2h = _s3("m_w2h", [P, 16 * 1024], BF16)
            mst0 = _s3("m_st0", [P, 2048]); mst1 = _s3("m_st1", [P, 2048])
            s_b1 = _s3("m_b1", [P, 32])
            s_h = _s3("m_h", [P, 16 * 512], BF16)
            s_lg = _s3("m_lg", [P, 16]); s_sc = _s3("m_sc", [P, 24])
            s_ct = _s3("m_ct", [NEXP, P], BF16)
            s_ym = _s3("m_y", [P, 2048])
            if True:
                s_mxn = [mxn0, mxn1]
                s_mst = [mst0, mst1]

                fn_c = em.c["fn"]
                for tt in range(NT512):
                    @blk.sync
                    def _(sy, tt=tt, fn_c=fn_c):
                        if tt >= 2:
                            sy.wait_ge(em.s["fn"], fn_c + 16 * (tt - 1))
                        em.inc(sy.dma_start(
                            out=s_mxn[tt % 2][:].rearrange("p (c t) -> p c t", c=8),
                            in_=ap["xnb"].rearrange("(c p) t -> p c t", p=P)[:, :, tt * 512:(tt + 1) * 512]), "ld", 16)
                    ld_t = em.c["ld"]
                    for tk in range(4):
                        ti = tt * 4 + tk

                        @blk.tensor
                        def _(t, tt=tt, tk=tk, ld_t=ld_t, ti=ti):
                            t.wait_ge(em.s["ld"], ld_t)
                            em.wait(t, "dv")
                            for dc in range(8):
                                inst = t.matmul(pA[:, 0:NEXP],
                                                s_mxn[tt % 2][:, dc * 512 + tk * P:dc * 512 + (tk + 1) * P],
                                                s_wgm[:, dc * NEXP:(dc + 1) * NEXP],
                                                start=(dc == 0), stop=(dc == 7))
                            if dbg and ti == 0:
                                t.matmul(pE[:, 0:NEXP], s_mxn[tt % 2][:, 0:P],
                                         s_wgm[:, 0:NEXP], start=True, stop=True)
                            em.inc(t.drain(), "tn", 1)

                        @blk.vector
                        def _(v, ti=ti, tt=tt):
                            em.wait(v, "tn")
                            em.wait(v, "dv")
                            if dbg and ti == 0:
                                v.tensor_copy(s_dbgw[:, 128:136], pA[:, 0:NEXP])
                                v.tensor_copy(s_dbgw[:, 136:144], pE[:, 0:NEXP])
                                v.tensor_copy(s_dbgw[:, 256:768],
                                              s_mxn[tt % 2][:, 3 * 512:4 * 512])
                            inst = v.tensor_tensor(out=s_lg[:, 0:8], in0=pA[:, 0:NEXP], in1=s_bgmb[:], op=ALU.add)
                            em.inc(inst, "dv", 1)
                            em.wait(v, "dv")
                            v.tensor_copy(s_dbgl[:, ti * 8:(ti + 1) * 8], s_lg[:, 0:8])
                            inst = v.tensor_reduce(out=s_lg[:, 8:9], in_=s_lg[:, 0:8], axis=AX.X,
                                                   op=ALU.max, negate=True)
                            em.inc(inst, "dv", 1)
                            em.wait(v, "dv")
                            inst = v.tensor_scalar(out=s_sc[:, 0:8], in0=s_lg[:, 0:8],
                                                   scalar1=s_lg[:, 8:9], op0=ALU.add,
                                                   scalar2=0.0, op1=ALU.is_ge)
                            em.inc(inst, "dv", 1)
                            em.wait(v, "dv")
                            inst = v.scalar_tensor_tensor(out=s_sc[:, 0:8], in0=s_sc[:, 0:8],
                                                          scalar=NEG, in1=s_lg[:, 0:8], op0=ALU.mult, op1=ALU.add)
                            em.inc(inst, "dv", 1)
                            em.wait(v, "dv")
                            inst = v.tensor_reduce(out=s_lg[:, 10:11], in_=s_sc[:, 0:8], axis=AX.X,
                                                   op=ALU.max, negate=True)
                            em.inc(inst, "dv", 1)
                            em.wait(v, "dv")
                            inst = v.tensor_scalar(out=s_sc[:, 8:16], in0=s_lg[:, 0:8],
                                                   scalar1=s_lg[:, 10:11], op0=ALU.add,
                                                   scalar2=0.0, op1=ALU.is_ge)
                            em.inc(inst, "dv", 1)

                        @blk.scalar
                        def _(s):
                            em.wait(s, "dv")
                            em.wait(s, "ac")
                            inst = s.activation(out=s_sc[:, 16:24], in_=s_lg[:, 0:8], func=AF.Exp,
                                                bias=s_lg[:, 8:9])
                            em.inc(inst, "ac", 1)

                        @blk.vector
                        def _(v, ti=ti):
                            em.wait(v, "ac")
                            em.wait(v, "dv")
                            inst = v.tensor_tensor(out=s_sc[:, 16:24], in0=s_sc[:, 16:24],
                                                   in1=s_sc[:, 8:16], op=ALU.mult)
                            em.inc(inst, "dv", 1)
                            em.wait(v, "dv")
                            inst = v.tensor_reduce(out=s_lg[:, 11:12], in_=s_sc[:, 16:24], axis=AX.X, op=ALU.add)
                            em.inc(inst, "dv", 1)
                            em.wait(v, "dv")
                            inst = v.reciprocal(out=s_lg[:, 14:15], in_=s_lg[:, 11:12])
                            em.inc(inst, "dv", 1)
                            em.wait(v, "dv")
                            inst = v.tensor_scalar(out=s_comb[:, ti * 8:(ti + 1) * 8], in0=s_sc[:, 16:24],
                                                   scalar1=s_lg[:, 14:15], scalar2=None, op0=ALU.mult)
                            em.inc(inst, "dv", 1)
                            if dbg and ti == 4:
                                em.wait(v, "dv")
                                v.tensor_copy(s_dbgw[:, 0:16], s_lg[:])
                                v.tensor_copy(s_dbgw[:, 16:40], s_sc[:])

                        @blk.tensor
                        def _(t, ti=ti):
                            em.wait(t, "dv")
                            inst = t.transpose(pB[0:NEXP, 0:P], s_comb[:, ti * 8:(ti + 1) * 8], identity=s_id[:])
                            em.inc(t.drain(), "tn", 1)

                        @blk.vector
                        def _(v):
                            em.wait(v, "tn")
                            em.wait(v, "dv")
                            inst = v.tensor_copy(s_ct[:], pB[0:NEXP, 0:P])
                            em.inc(inst, "dv", 1)

                        @blk.tensor
                        def _(t):
                            em.wait(t, "dv")
                            t.matmul(pC[:, 0:512], s_ct[:], s_b2[:, 0:512], start=True, stop=True)
                            inst = t.matmul(pD[:, 0:512], s_ct[:], s_b2[:, 512:1024], start=True, stop=True)
                            em.inc(t.drain(), "tn", 1)

                        @blk.vector
                        def _(v):
                            em.wait(v, "tn")
                            em.wait(v, "dv")
                            em.wait(v, "st")
                            em.wait(v, "fn")
                            v.tensor_copy(s_ym[:, 0:512], pC[:, 0:512])
                            inst = v.tensor_copy(s_ym[:, 512:1024], pD[:, 0:512])
                            em.inc(inst, "dv", 1)

                        @blk.gpsimd
                        def _(gq, ti=ti, tk=tk):
                            em.wait(gq, "dv")
                            inst = gq.dma_start(out=ap["acc"][ti * P:(ti + 1) * P, :], in_=s_ym[:, 0:1024])
                            if tk == 3:
                                em.inc(inst, "fn", 16)
                            else:
                                em.inc(inst, "st", 16)

                for ei, e in enumerate([0, 2, 4, 6, 1, 3, 5, 7][:n_exp_run]):
                    act_fn = AF.Gelu if e % 2 == 0 else AF.Silu
                    for half in range(2):
                        f0 = half * 16
                        w1v = ap["w1"].rearrange("e p (c f) -> e p c f", c=8)
                        cst0 = em.c["cst"]
                        for sl in range(8):
                            @blk.sync
                            def _(sy, e=e, half=half, sl=sl, cst0=cst0):
                                if cst0 + sl - 1 > 0:
                                    sy.wait_ge(em.s["cst"], cst0 + sl - 1)
                                em.inc(sy.dma_start(
                                    out=s_mst[sl % 2][:].rearrange("p (c f) -> p c f", c=8)[:, :, 0:256],
                                    in_=w1v[e, :, :, half * 2048 + sl * 256:half * 2048 + (sl + 1) * 256]), "ld", 16)

                            @blk.vector
                            def _(v, sl=sl):
                                em.wait(v, "ld")
                                if sl == 0:
                                    em.wait(v, "tn")
                                inst = v.tensor_copy(
                                    s_w1h[:].rearrange("p (c f) -> p c f", c=8)[:, :, sl * 256:(sl + 1) * 256],
                                    s_mst[sl % 2][:].rearrange("p (c f) -> p c f", c=8)[:, :, 0:256])
                                em.inc(inst, "cst", 1)
                        cst1 = em.c["cst"]
                        for sl in range(8):
                            @blk.sync
                            def _(sy, e=e, f0=f0, sl=sl, cst1=cst1):
                                sy.wait_ge(em.s["cst"], cst1 + sl - 1)
                                em.inc(sy.dma_start(
                                    out=s_mst[sl % 2][:, 0:2048],
                                    in_=ap["w2"][e, :, (f0 + sl * 2) * 1024:(f0 + sl * 2 + 2) * 1024]), "ld", 16)

                            @blk.vector
                            def _(v, sl=sl):
                                em.wait(v, "ld")
                                inst = v.tensor_copy(s_w2h[:, sl * 2048:(sl + 1) * 2048],
                                                     s_mst[sl % 2][:, 0:2048])
                                em.inc(inst, "cst", 1)

                        @blk.sync
                        def _(sy, e=e, half=half):
                            em.inc(sy.dma_start(out=s_b1[:, half * 16:(half + 1) * 16],
                                                in_=ap["b1c"][e][:, half * 16:(half + 1) * 16]), "ld", 16)

                        fn_e = em.c["fn"]
                        for tt in range(NT512):
                            @blk.sync
                            def _(sy, tt=tt, fn_e=fn_e):
                                if tt >= 2:
                                    sy.wait_ge(em.s["fn"], fn_e + 16 * (tt - 1))
                                em.inc(sy.dma_start(
                                    out=s_mxn[tt % 2][:].rearrange("p (c t) -> p c t", c=8),
                                    in_=ap["xnb"].rearrange("(c p) t -> p c t", p=P)[:, :, tt * 512:(tt + 1) * 512]), "ld", 16)
                            ld_t = em.c["ld"]
                            tn0 = em.c["tn"]
                            ac0 = em.c["ac"]

                            @blk.tensor
                            def _(t, tt=tt, ld_t=ld_t, ac0=ac0):
                                t.wait_ge(em.s["ld"], ld_t)
                                em.wait(t, "cst")
                                em.wait(t, "dv")
                                em.wait(t, "ac")
                                for fc in range(16):
                                    if fc >= 2:
                                        t.wait_ge(em.s["ac"], ac0 + fc - 1)
                                    pp = pA if fc % 2 == 0 else pB
                                    for dc in range(8):
                                        inst = t.matmul(pp[:, 0:512],
                                                        s_w1h[:, dc * 2048 + fc * P:dc * 2048 + (fc + 1) * P],
                                                        s_mxn[tt % 2][:, dc * 512:(dc + 1) * 512],
                                                        start=(dc == 0), stop=(dc == 7))
                                    em.inc(t.drain(), "tn", 1)

                            @blk.scalar
                            def _(s, half=half, act_fn=act_fn, tn0=tn0):
                                em.wait(s, "ac")
                                for fc in range(16):
                                    s.wait_ge(em.s["tn"], tn0 + fc + 1)
                                    pp = pA if fc % 2 == 0 else pB
                                    inst = s.activation(out=s_h[:, fc * 512:(fc + 1) * 512], in_=pp[:, 0:512],
                                                        func=act_fn, bias=s_b1[:, half * 16 + fc:half * 16 + fc + 1])
                                    em.inc(inst, "ac", 1)

                            tn1 = em.c["tn"]
                            dv0 = em.c["dv"]

                            @blk.tensor
                            def _(t, dv0=dv0):
                                em.wait(t, "ac")
                                for j in range(8):
                                    tk, dh = j // 2, j % 2
                                    if j >= 2:
                                        t.wait_ge(em.s["dv"], dv0 + j - 1)
                                    pp = pC if j % 2 == 0 else pD
                                    for fc in range(16):
                                        inst = t.matmul(pp[:, 0:512],
                                                        s_h[:, fc * 512 + tk * P:fc * 512 + (tk + 1) * P],
                                                        s_w2h[:, fc * 1024 + dh * 512:fc * 1024 + (dh + 1) * 512],
                                                        start=(fc == 0), stop=(fc == 15))
                                    em.inc(t.drain(), "tn", 1)

                            st0 = em.c["st"]
                            dvp = em.c["dv"]

                            @blk.vector
                            def _(v, tt=tt, e=e, tn1=tn1, st0=st0, ei=ei, half=half):
                                em.wait(v, "dv")
                                if dbg and ei == 0 and half == 0 and tt == 0:
                                    v.wait_ge(em.s["tn"], tn1 + 1)
                                    v.tensor_copy(s_dbgw[:, 512:640], s_w2h[:, 0:128])
                                    v.tensor_copy(s_dbgw[:, 640:768], s_h[:, 0:128])
                                for j in range(8):
                                    tk, dh = j // 2, j % 2
                                    ti = tt * 4 + tk
                                    buf = (tk % 2) * 1024
                                    v.wait_ge(em.s["tn"], tn1 + j + 1)
                                    if dh == 0 and tk >= 2:
                                        v.wait_ge(em.s["st"], st0 + 16 * (tk - 1))
                                    pp = pC if j % 2 == 0 else pD
                                    inst = v.tensor_scalar(out=s_ym[:, buf + dh * 512:buf + (dh + 1) * 512],
                                                           in0=pp[:, 0:512],
                                                           scalar1=s_comb[:, ti * 8 + e:ti * 8 + e + 1],
                                                           scalar2=None, op0=ALU.mult)
                                    em.inc(inst, "dv", 1)
                                    if dbg and ei == 0 and half == 0 and tt == 0 and j == 0:
                                        v.tensor_copy(s_dbgw[:, 256:384], pp[:, 0:128])
                                        v.tensor_copy(s_dbgw[:, 384:512], s_h[:, 15 * 512:15 * 512 + 128])

                            @blk.gpsimd
                            def _(gq, tt=tt, fn_e=fn_e, dvp=dvp, st0=st0):
                                for tk in range(4):
                                    ti = tt * 4 + tk
                                    buf = (tk % 2) * 1024
                                    gq.wait_ge(em.s["dv"], dvp + 2 * (tk + 1))
                                    inst = gq.dma_start(out=ap["acc"][ti * P:(ti + 1) * P, :],
                                                        in_=s_ym[:, buf:buf + 1024],
                                                        accum_op=ALU.add)
                                    if tk == 3:
                                        em.inc(inst, "fn", 16)
                                    else:
                                        em.inc(inst, "st", 16)

            full_barrier(blk, em)
            ph3.close()

            ph4 = ExitStack()
            _s4l = lambda n, sh, d=F32: ph4.enter_context(nc.sbuf_tensor(n, sh, d))
            s_fa = _s4l("f_a", [P, D]); s_fb = _s4l("f_b", [P, D]); s_fc = _s4l("f_c", [P, D])
            f_mv = _s4l("f_mv", [P, 12]); f_ms = _s4l("f_ms", [P, 8])
            s_gcp = _s4l("f_gc", [1, D])
            if True:
                for sq in range(nseq):
                    for tl in range(T // P):
                        ti = sq * (T // P) + tl

                        @blk.sync
                        def _(sy, ti=ti):
                            if ti >= 1:
                                sy.wait_ge(em.s["fn"], em.c["fn"])
                                em.wait(sy, "tn")
                            em.inc(sy.dma_start(out=s_fa[:], in_=ap["x"][ti * P:(ti + 1) * P, :]), "ld", 16)
                            em.inc(sy.dma_start(out=s_fb[:], in_=ap["yatt"][ti * P:(ti + 1) * P, :]), "ld", 16)
                            em.inc(sy.dma_start(out=s_fc[:], in_=ap["acc"][ti * P:(ti + 1) * P, :]), "ld", 16)

                        @blk.vector
                        def _(v):
                            em.wait(v, "ld")
                            em.wait(v, "dv")
                            v.tensor_tensor(out=s_fa[:], in0=s_fa[:], in1=s_fb[:], op=ALU.add)
                            inst = v.tensor_tensor(out=s_fa[:], in0=s_fa[:], in1=s_fc[:], op=ALU.add)
                            em.inc(inst, "dv", 1)
                            v.bn_stats(f_mv[:, 0:6], s_fa[:, 0:512])
                            inst = v.bn_stats(f_mv[:, 6:12], s_fa[:, 512:1024])
                            em.inc(inst, "dv", 1)
                            em.wait(v, "dv")
                            inst = v.bn_aggr(f_ms[:, 0:2], f_mv[:, 0:12])
                            em.inc(inst, "dv", 1)
                            em.wait(v, "dv")
                            inst = v.tensor_scalar(out=f_ms[:, 2:3], in0=f_ms[:, 0:1],
                                                   scalar1=-1.0, scalar2=None, op0=ALU.mult)
                            em.inc(inst, "dv", 1)

                        @blk.scalar
                        def _(s):
                            em.wait(s, "dv")
                            em.wait(s, "ac")
                            s.activation(out=f_ms[:, 3:4], in_=f_ms[:, 1:2], func=AF.Ln, bias=s_cb[:, 0:1])
                            inst = s.activation(out=f_ms[:, 3:4], in_=f_ms[:, 3:4], func=AF.Exp, scale=-0.5)
                            em.inc(inst, "ac", 1)

                        @blk.vector
                        def _(v):
                            em.wait(v, "ac")
                            em.wait(v, "dv")
                            inst = v.tensor_scalar(out=s_fb[:], in0=s_fa[:],
                                                   scalar1=f_ms[:, 2:3], op0=ALU.add,
                                                   scalar2=f_ms[:, 3:4], op1=ALU.mult)
                            em.inc(inst, "dv", 1)

                        @blk.gpsimd
                        def _(gq, ti=ti):
                            em.wait(gq, "dv")
                            inst = gq.dma_start(out=ap["x1"][ti * P:(ti + 1) * P, :], in_=s_fa[:])
                            em.inc(inst, "fn", 16)

                        @blk.tensor
                        def _(t, tl=tl):
                            em.wait(t, "dv")
                            t.matmul(pA[0:1, 0:512], s_ones[:], s_fb[:, 0:512],
                                     start=(tl == 0), stop=(tl == T // P - 1))
                            inst = t.matmul(pB[0:1, 0:512], s_ones[:], s_fb[:, 512:1024],
                                            start=(tl == 0), stop=(tl == T // P - 1))
                            em.inc(t.drain(), "tn", 1)

                    @blk.vector
                    def _(v):
                        em.wait(v, "tn")
                        em.wait(v, "dv")
                        v.tensor_scalar(out=s_gcp[0:1, 0:512], in0=pA[0:1, 0:512],
                                        scalar1=1.0 / T, scalar2=None, op0=ALU.mult)
                        inst = v.tensor_scalar(out=s_gcp[0:1, 512:1024], in0=pB[0:1, 0:512],
                                               scalar1=1.0 / T, scalar2=None, op0=ALU.mult)
                        em.inc(inst, "dv", 1)

                    @blk.gpsimd
                    def _(gq, sq=sq):
                        em.wait(gq, "dv")
                        inst = gq.dma_start(out=ap["gc"][sq:sq + 1, :], in_=s_gcp[:])
                        em.inc(inst, "st", 16)

            ph4.close()
        if dbg:
            @blk.gpsimd
            def _(gq):
                em.wait(gq, "dv")
                em.wait(gq, "ac")
                em.inc(gq.dma_start(out=ap["dbg1"][:, 0:NINST], in_=s_lacol[:]), "st", 16)
                em.inc(gq.dma_start(out=ap["dbg1"][:, NINST:2 * NINST], in_=s_bcol[:]), "st", 16)
                em.inc(gq.dma_start(out=ap["dbg1"][:, 4 * NINST:4 * NINST + 8 * NT128], in_=s_comb[:]), "st", 16)
                em.inc(gq.dma_start(out=ap["dbg2"], in_=s_dbgl[:]), "st", 16)
                em.inc(gq.dma_start(out=ap["dbg3"], in_=s_dbgw[:]), "st", 16)

        @blk.sync
        def _(sy):
            em.wait(sy, "st")
            em.wait(sy, "fn")

        em.close()
    return nc



def l0_inputs(x_core, d):
    f = np.float32
    g = lambda n: np.asarray(d[n], f)
    W1 = g("L0_ffn_W1")
    W2 = g("L0_ffn_W2")
    w1 = np.ascontiguousarray(W1.reshape(NEXP, 8, P, DFF).transpose(0, 2, 1, 3).reshape(NEXP, P, 8 * DFF))
    w2 = np.ascontiguousarray(W2.reshape(NEXP, 32, P, D).transpose(0, 2, 1, 3).reshape(NEXP, P, 32 * D))
    b1c = np.ascontiguousarray(g("L0_ffn_b1").reshape(NEXP, 32, P).transpose(0, 2, 1))
    m_sl = np.where(np.tril(np.ones((P, P)), -1) > 0, 0.0, NEG).astype(f)
    m_us = np.where(np.triu(np.ones((P, P)), 1) > 0, 0.0, NEG).astype(f)
    m_ui = np.where(np.triu(np.ones((P, P)), 0) > 0, 0.0, NEG).astype(f)
    return {
        "x": np.ascontiguousarray(x_core),
        "wq": g("L0_attn_Wq"), "wk": g("L0_attn_Wk"), "wv": g("L0_attn_Wv"),
        "wga": g("L0_attn_Wg"), "wa": g("L0_attn_Wa"), "wb": g("L0_attn_Wb"),
        "wo": g("L0_attn_Wo"),
        "bq_c": g("L0_attn_bq").reshape(P, 1),
        "bk_b": np.tile(g("L0_attn_bk").reshape(1, CC), (P, 1)),
        "bv_b": np.tile(g("L0_attn_bv").reshape(1, CC), (P, 1)),
        "bg_b": np.tile(g("L0_attn_bg").reshape(1, CC), (P, 1)),
        "ab_c": np.array([[g("L0_attn_ba")[0], g("L0_attn_bb")[0]]], f),
        "bo_b": np.tile(g("L0_attn_bo").reshape(1, D), (P, 1)),
        "w1": w1, "b1c": b1c, "w2": w2, "b2r": g("L0_ffn_b2"),
        "wgm": g("L0_ffn_Wg"),
        "bgm_b": np.tile(g("L0_ffn_bg").reshape(1, NEXP), (P, 1)),
        "ident": np.eye(P, dtype=f),
        "uptri": np.triu(np.ones((P, P), f)),
        "ones_c": np.ones((P, 1), f),
        "m_sl": m_sl, "m_us": m_us, "m_ui": m_ui,
    }





def build_l1(npos, ph=99):
    ntok = npos * 32
    NT128 = ntok // P
    KD, NKEY, TOT = 16, 256, 48
    nc = bass.Bass("TRN2", target_bir_lowering=False)
    dt = nc.dram_tensor
    ap = {}
    for n, sh in [("x1", [ntok, D]), ("gcin", [32, D]),
                  ("wqt", [P, 8 * 768]), ("bq6", [P, 6]), ("gq_c", [P, 1]), ("bqln_c", [P, 1]),
                  ("krows", [TOT * NKEY, KD]), ("vals", [P, TOT * 2 * 32]),
                  ("bdwres", [P, 12 * P]), ("bdwmem", [P, 24 * P]),
                  ("bres_c", [P, 1]), ("bmem_c", [P, 1]),
                  ("wot", [P, 8 * D]), ("gout_b", [P, D]), ("bout_b", [P, D]),
                  ("cn1", [P, 8 * D]), ("cn2", [P, 8 * 512]), ("cn3", [P, 4 * 256]),
                  ("cn4", [P, 2]), ("cb1", [P, 8]), ("cb2", [P, 4]), ("cb3", [P, 2]), ("cb4_c", [P, 1]),
                  ("gc1", [P, 8 * 512]), ("gc2", [P, 4 * 256]), ("gc3", [P, 2]),
                  ("gb1_b", [32, 512]), ("gb2_b", [32, 256]), ("gb3_b", [32, 1]), ("arr", [1, 1]),
                  ("bdg8", [P, 8]), ("bdg8t", [8, P]),
                  ("ident", [P, P]), ("blkmask", [P, P]), ("ones_c", [P, 1])]:
        ap[n] = dt(n, sh, F32, kind="ExternalInput").ap()
    ap["x2"] = dt("x2", [ntok, D], F32, kind="ExternalOutput").ap()
    for n, sh, d in [("xn1", [ntok, D], F32), ("xnf", [D, ntok], BF16),
                     ("hqf", [1024, ntok], BF16), ("hkf", [1024, ntok], BF16),
                     ("hvt", [ntok, 1024], BF16), ("lof", [1024, ntok], BF16),
                     ("atn", [ntok, D], F32), ("ffs", [ntok, D], F32)]:
        ap[n] = dt(n, sh, d).ap()

    es = ExitStack()
    sb = lambda n, sh, d=F32: es.enter_context(nc.sbuf_tensor(n, sh, d))
    s_wq = sb("t_wq", [P, 8 * 768], BF16)
    s_wo = sb("t_wo", [P, 8 * D], BF16)
    s_bdk = sb("t_bdk", [P, 6 * 2048], BF16)
    s_vals = sb("t_vals", [P, TOT * 2 * 32], BF16)
    s_bwr = sb("t_bwr", [P, 12 * P], BF16)
    s_bwm = sb("t_bwm", [P, 24 * P], BF16)
    s_cn1 = sb("t_cn1", [P, 8 * D], BF16)
    s_cn2 = sb("t_cn2", [P, 8 * 512], BF16)
    s_cn3 = sb("t_cn3", [P, 4 * 256], BF16)
    s_cn4 = sb("t_cn4", [P, 2], BF16)
    s_gc1 = sb("t_gc1", [P, 8 * 512], BF16)
    s_gc2 = sb("t_gc2", [P, 4 * 256], BF16)
    s_gc3 = sb("t_gc3", [P, 2], BF16)
    s_bq6 = sb("t_bq6", [P, 6])
    s_gq = sb("t_gq", [P, 2])
    s_bres = sb("t_bres", [P, 2])
    s_cb1 = sb("t_cb1", [P, 8])
    s_cb2 = sb("t_cb2", [P, 4])
    s_cb3 = sb("t_cb3", [P, 2])
    s_cb4 = sb("t_cb4", [P, 1])
    s_gb1 = sb("t_gb1", [32, 512])
    s_gb2 = sb("t_gb2", [32, 256])
    s_gb3 = sb("t_gb3", [32, 1])
    s_arr = sb("t_arr", [1, 1])
    s_gout = sb("t_gout", [P, D])
    s_bout = sb("t_bout", [P, D])
    s_bdg = sb("t_bdg", [P, 8])
    s_bdgt = sb("t_bdgt", [8, P])
    s_id = sb("t_id", [P, P])
    s_idb = sb("t_idb", [P, P], BF16)
    s_blk = sb("t_blk", [P, P])
    s_ones = sb("t_ones", [P, 1])
    s_onesb = sb("t_onesb", [P, 1], BF16)
    s_gfb = sb("t_gfb", [P, 4])
    s_onesr = sb("t_onesr", [1, P])
    s_stage = sb("t_stage", [P, 4096])
    s_eps = sb("t_eps", [P, 2])

    pf = [es.enter_context(nc.psum_tensor(f"qf{i}", [P, 512], F32)) for i in range(7)]
    pA, pB, pC, pD, pE, pF, pG = pf
    pT = es.enter_context(nc.psum_tensor("qT", [P, 1024], BF16))

    em = Em(nc, ["ld", "st", "tn", "dv", "ac", "gp", "fn", "sx", "cst"])

    def rt(v, inst):
        em.inc(inst, "dv", 1)
        em.wait(v, "dv")

    with es, nc.Block() as blk:
        @blk.sync
        def _(sy):
            for dst, src in [(s_bq6, "bq6"), (s_gout, "gout_b"), (s_bout, "bout_b"),
                             (s_bdg, "bdg8"), (s_bdgt, "bdg8t"), (s_id, "ident"),
                             (s_blk, "blkmask"), (s_ones, "ones_c"),
                             (s_cb1, "cb1"), (s_cb2, "cb2"), (s_cb3, "cb3"), (s_cb4, "cb4_c"),
                             (s_gb1, "gb1_b"), (s_gb2, "gb2_b"), (s_gb3, "gb3_b"), (s_arr, "arr")]:
                em.inc(sy.dma_start(out=dst[:], in_=ap[src]), "ld", 16)
            em.inc(sy.dma_start(out=s_gq[:, 0:1], in_=ap["gq_c"]), "ld", 16)
            em.inc(sy.dma_start(out=s_gq[:, 1:2], in_=ap["bqln_c"]), "ld", 16)
            em.inc(sy.dma_start(out=s_bres[:, 0:1], in_=ap["bres_c"]), "ld", 16)
            em.inc(sy.dma_start(out=s_bres[:, 1:2], in_=ap["bmem_c"]), "ld", 16)

        jobs = []
        for j0 in range(3):
            jobs.append((ap["wqt"][:, j0 * 2048:(j0 + 1) * 2048], s_wq[:, j0 * 2048:(j0 + 1) * 2048], 2048, P))
        for j0 in range(4):
            jobs.append((ap["wot"][:, j0 * 2048:(j0 + 1) * 2048], s_wo[:, j0 * 2048:(j0 + 1) * 2048], 2048, P))
        jobs.append((ap["vals"][:, 0:1536], s_vals[:, 0:1536], 1536, P))
        jobs.append((ap["vals"][:, 1536:3072], s_vals[:, 1536:3072], 1536, P))
        jobs.append((ap["bdwres"], s_bwr[:], 12 * P, P))
        jobs.append((ap["bdwmem"][:, 0:12 * P], s_bwm[:, 0:12 * P], 12 * P, P))
        jobs.append((ap["bdwmem"][:, 12 * P:24 * P], s_bwm[:, 12 * P:24 * P], 12 * P, P))
        for j0 in range(4):
            jobs.append((ap["cn1"][:, j0 * 2048:(j0 + 1) * 2048], s_cn1[:, j0 * 2048:(j0 + 1) * 2048], 2048, P))
        for j0 in range(2):
            jobs.append((ap["cn2"][:, j0 * 2048:(j0 + 1) * 2048], s_cn2[:, j0 * 2048:(j0 + 1) * 2048], 2048, P))
        jobs.append((ap["cn3"], s_cn3[:], 4 * 256, P))
        jobs.append((ap["cn4"], s_cn4[:], 2, P))
        for j0 in range(2):
            jobs.append((ap["gc1"][:, j0 * 2048:(j0 + 1) * 2048], s_gc1[:, j0 * 2048:(j0 + 1) * 2048], 2048, P))
        jobs.append((ap["gc2"], s_gc2[:], 4 * 256, P))
        jobs.append((ap["gc3"], s_gc3[:], 2, P))
        for j, (src, dst, width, rows) in enumerate(jobs):
            off = (j % 2) * 2048

            @blk.sync
            def _(sy, src=src, off=off, j=j, width=width, rows=rows):
                if j >= 2:
                    em.wait(sy, "cst", j - 1)
                em.inc(sy.dma_start(out=s_stage[0:rows, off:off + width], in_=src), "ld", 16)

            @blk.vector
            def _(v, dst=dst, off=off, width=width, rows=rows):
                em.wait(v, "ld")
                em.inc(v.tensor_copy(dst, s_stage[0:rows, off:off + width]), "cst", 1)

        @blk.vector
        def _(v):
            em.wait(v, "cst")
            em.wait(v, "ld")
            v.tensor_copy(s_idb[:], s_id[:])
            v.tensor_copy(s_onesb[:], s_ones[:])
            v.memset(s_onesr[:], 1.0)
            v.memset(s_eps[:, 0:1], 1e-5)
            v.memset(s_eps[:, 1:2], 1.0)
            for g6 in range(6):
                inst = v.memset(s_bdk[:, g6 * 2048:(g6 + 1) * 2048], 0)
            em.inc(inst, "dv", 1)

        with nc.sbuf_tensor("kr", [P, KD], F32) as s_kr, \
             nc.sbuf_tensor("krb", [16, P], BF16) as s_krb, \
             nc.sbuf_tensor("kc", [P, 8], F32) as s_kc, \
             nc.sbuf_tensor("ksq", [P, KD], F32) as s_ksq:
            for kt in range(TOT * NKEY // P):
                @blk.sync
                def _(sy, kt=kt):
                    if kt >= 1:
                        em.wait(sy, "gp")
                    em.inc(sy.dma_start(out=s_kr[:], in_=ap["krows"][kt * P:(kt + 1) * P, :]), "ld", 16)

                @blk.scalar
                def _(s):
                    em.wait(s, "ld")
                    em.wait(s, "ac")
                    s.activation(out=s_ksq[:], in_=s_kr[:], func=AF.Square, accum_out=s_kc[:, 0:1])
                    inst = s.activation(out=s_kc[:, 6:7], in_=s_eps[:, 1:2], func=AF.Identity)
                    em.inc(inst, "ac", 1)

                @blk.vector
                def _(v):
                    em.wait(v, "ac")
                    em.wait(v, "dv")
                    inst = v.tensor_scalar(out=s_kc[:, 1:2], in0=s_kc[:, 0:1], scalar1=1e-20,
                                           scalar2=None, op0=ALU.max)
                    rt(v, inst)

                @blk.scalar
                def _(s):
                    em.wait(s, "dv")
                    em.wait(s, "ac")
                    s.activation(out=s_kc[:, 2:3], in_=s_kc[:, 1:2], func=AF.Ln)
                    inst = s.activation(out=s_kc[:, 2:3], in_=s_kc[:, 2:3], func=AF.Exp, scale=-0.5)
                    em.inc(inst, "ac", 1)

                @blk.vector
                def _(v):
                    em.wait(v, "ac")
                    em.wait(v, "dv")
                    inst = v.tensor_scalar(out=s_kr[:], in0=s_kr[:], scalar1=s_kc[:, 2:3],
                                           scalar2=None, op0=ALU.mult)
                    rt(v, inst)

                @blk.tensor
                def _(t):
                    em.wait(t, "dv")
                    t.transpose(pG[0:KD, 0:P], s_kr[:, 0:KD] if False else s_kr[:], identity=s_id[:])
                    em.inc(t.drain(), "tn", 1)

                @blk.vector
                def _(v):
                    em.wait(v, "tn")
                    em.wait(v, "dv")
                    inst = v.tensor_copy(s_krb[:], pG[0:KD, 0:P])
                    em.inc(inst, "dv", 1)

                @blk.gpsimd
                def _(gq, kt=kt):
                    em.wait(gq, "dv")
                    row0 = kt * P
                    s0 = row0 // NKEY
                    k0 = row0 % NKEY
                    g6 = s0 // 8
                    lane = s0 % 8
                    inst = gq.dma_start(
                        out=s_bdk[16 * lane:16 * lane + 16,
                                  g6 * 2048 + lane * 256 + k0:g6 * 2048 + lane * 256 + k0 + P],
                        in_=s_krb[:])
                    em.inc(inst, "gp", 16)

        full_barrier(blk, em)

        if ph <= 0:
            @blk.sync
            def _(sy):
                em.wait(sy, "st")
                em.wait(sy, "gp")
            em.close()
            return nc

        with nc.sbuf_tensor("g_a", [32, D], F32) as s_ga, nc.sbuf_tensor("g_f", [P, 256], F32) as s_gf, \
             nc.sbuf_tensor("g_t", [32, 512], F32) as s_gt, nc.sbuf_tensor("g_t2", [P, 128], BF16) as s_gt2, \
             nc.sbuf_tensor("g_t3", [32, 256], F32) as s_gt3, nc.sbuf_tensor("g_o", [32, 1], F32) as s_go:
            @blk.sync
            def _(sy):
                em.inc(sy.dma_start(out=s_ga[:], in_=ap["gcin"]), "ld", 16)

            @blk.tensor
            def _(t):
                em.wait(t, "ld")
                em.wait(t, "dv")
                for dc in range(8):
                    t.transpose(pG[:, dc * 32:(dc + 1) * 32], s_ga[:, dc * P:(dc + 1) * P],
                                identity=s_id[0:32, 0:32])
                em.inc(t.drain(), "tn", 1)

            @blk.vector
            def _(v):
                em.wait(v, "tn")
                em.wait(v, "dv")
                inst = v.tensor_copy(s_gf[:], pG[:, 0:256])
                rt(v, inst)
                inst = v.tensor_copy(s_stage[:, 0:256], s_gf[:])
                em.inc(inst, "dv", 1)

            @blk.vector
            def _(v):
                em.wait(v, "dv")
                inst = v.tensor_copy(s_gt2[:], s_gf[:, 0:128])
                em.inc(inst, "dv", 1)
            with nc.sbuf_tensor("g_fb", [P, 256], BF16) as s_gfb2:
                @blk.vector
                def _(v):
                    em.wait(v, "dv")
                    inst = v.tensor_copy(s_gfb2[:], s_gf[:])
                    em.inc(inst, "dv", 1)

                @blk.tensor
                def _(t):
                    em.wait(t, "dv")
                    for dc in range(8):
                        t.matmul(pA[0:32, 0:512], s_gfb2[:, dc * 32:(dc + 1) * 32],
                                 s_gc1[:, dc * 512:(dc + 1) * 512], start=(dc == 0), stop=(dc == 7))
                    em.inc(t.drain(), "tn", 1)

                @blk.vector
                def _(v):
                    em.wait(v, "tn")
                    em.wait(v, "dv")
                    inst = v.tensor_tensor(out=s_gt[:], in0=pA[0:32, 0:512], in1=s_gb1[:], op=ALU.add)
                    em.inc(inst, "dv", 1)

                @blk.scalar
                def _(s):
                    em.wait(s, "dv")
                    em.wait(s, "ac")
                    inst = s.activation(out=s_gt[:], in_=s_gt[:], func=AF.Gelu)
                    em.inc(inst, "ac", 1)

                @blk.tensor
                def _(t):
                    em.wait(t, "ac")
                    for c4 in range(4):
                        t.transpose(pG[:, c4 * 32:(c4 + 1) * 32], s_gt[:, c4 * P:(c4 + 1) * P],
                                    identity=s_id[0:32, 0:32])
                    em.inc(t.drain(), "tn", 1)

                @blk.vector
                def _(v):
                    em.wait(v, "tn")
                    em.wait(v, "dv")
                    inst = v.tensor_copy(s_gfb2[:, 0:128], pG[:, 0:128])
                    em.inc(inst, "dv", 1)

                @blk.tensor
                def _(t):
                    em.wait(t, "dv")
                    for c4 in range(4):
                        t.matmul(pB[0:32, 0:256], s_gfb2[:, c4 * 32:(c4 + 1) * 32],
                                 s_gc2[:, c4 * 256:(c4 + 1) * 256], start=(c4 == 0), stop=(c4 == 3))
                    em.inc(t.drain(), "tn", 1)

                @blk.vector
                def _(v):
                    em.wait(v, "tn")
                    em.wait(v, "dv")
                    inst = v.tensor_tensor(out=s_gt3[:], in0=pB[0:32, 0:256], in1=s_gb2[:], op=ALU.add)
                    em.inc(inst, "dv", 1)

                @blk.scalar
                def _(s):
                    em.wait(s, "dv")
                    em.wait(s, "ac")
                    inst = s.activation(out=s_gt3[:], in_=s_gt3[:], func=AF.Gelu)
                    em.inc(inst, "ac", 1)

                @blk.tensor
                def _(t):
                    em.wait(t, "ac")
                    for c2 in range(2):
                        t.transpose(pG[:, 256 + c2 * 32:256 + (c2 + 1) * 32],
                                    s_gt3[:, c2 * P:(c2 + 1) * P], identity=s_id[0:32, 0:32])
                    em.inc(t.drain(), "tn", 1)

                @blk.vector
                def _(v):
                    em.wait(v, "tn")
                    em.wait(v, "dv")
                    inst = v.tensor_copy(s_gfb2[:, 128:192], pG[:, 256:320])
                    em.inc(inst, "dv", 1)

                @blk.tensor
                def _(t):
                    em.wait(t, "dv")
                    for c2 in range(2):
                        t.matmul(pC[0:32, 0:1], s_gfb2[:, 128 + c2 * 32:128 + (c2 + 1) * 32],
                                 s_gc3[:, c2:c2 + 1], start=(c2 == 0), stop=(c2 == 1))
                    em.inc(t.drain(), "tn", 1)

                @blk.vector
                def _(v):
                    em.wait(v, "tn")
                    em.wait(v, "dv")
                    inst = v.tensor_tensor(out=s_go[:], in0=pC[0:32, 0:1], in1=s_gb3[:], op=ALU.add)
                    rt(v, inst)

                @blk.scalar
                def _(s):
                    em.wait(s, "dv")
                    em.wait(s, "ac")
                    inst = s.activation(out=s_go[:], in_=s_go[:], func=AF.Sigmoid)
                    em.inc(inst, "ac", 1)

                @blk.gpsimd
                def _(gq):
                    em.wait(gq, "ac")
                    for r4 in range(4):
                        inst = gq.dma_start(out=s_gfb[32 * r4:32 * (r4 + 1), 0:1], in_=s_go[:])
                        em.inc(inst, "gp", 16)

                @blk.vector
                def _(v):
                    em.wait(v, "gp")
                    em.wait(v, "dv")
                    inst = v.tensor_scalar(out=s_gfb[:, 1:2], in0=s_gfb[:, 0:1], scalar1=0.7,
                                           scalar2=None, op0=ALU.mult)
                    em.inc(inst, "dv", 1)
                @blk.tensor
                def _(t):
                    em.wait(t, "dv")
                    t.matmul(pC[:, 64:65], s_onesr[:], s_arr[:], start=True, stop=True)
                    em.inc(t.drain(), "tn", 1)

                @blk.vector
                def _(v):
                    em.wait(v, "tn")
                    em.wait(v, "dv")
                    inst = v.tensor_copy(s_gfb[:, 2:3], pC[:, 64:65])
                    em.inc(inst, "dv", 1)

        full_barrier(blk, em)

        if ph <= 1:
            @blk.sync
            def _(sy):
                em.wait(sy, "st")
                em.wait(sy, "gp")
            em.close()
            return nc

        with nc.sbuf_tensor("b_x", [P, D], F32) as s_x, nc.sbuf_tensor("b_mv", [P, 12], F32) as s_mv, \
             nc.sbuf_tensor("b_ms", [P, 8], F32) as s_ms, nc.sbuf_tensor("b_xn", [P, D], F32) as s_xn, \
             nc.sbuf_tensor("b_xnb", [P, D], BF16) as s_xnb, \
             nc.sbuf_tensor("b_xf", [P, 8 * P], BF16) as s_xf:
            for ti in range(NT128):
                @blk.sync
                def _(sy, ti=ti):
                    if ti >= 1:
                        em.wait(sy, "st")
                    em.inc(sy.dma_start(out=s_x[:], in_=ap["x1"][ti * P:(ti + 1) * P, :]), "ld", 16)

                @blk.vector
                def _(v):
                    em.wait(v, "ld")
                    em.wait(v, "dv")
                    inst = v.bn_stats(s_mv[:, 0:6], s_x[:, 0:512])
                    inst = v.bn_stats(s_mv[:, 6:12], s_x[:, 512:1024])
                    rt(v, inst)
                    inst = v.bn_aggr(s_ms[:, 0:2], s_mv[:, 0:12])
                    rt(v, inst)
                    inst = v.tensor_scalar(out=s_ms[:, 2:3], in0=s_ms[:, 0:1],
                                           scalar1=-1.0, scalar2=None, op0=ALU.mult)
                    em.inc(inst, "dv", 1)

                @blk.scalar
                def _(s):
                    em.wait(s, "dv")
                    em.wait(s, "ac")
                    s.activation(out=s_ms[:, 3:4], in_=s_ms[:, 1:2], func=AF.Ln, bias=s_eps[:, 0:1])
                    inst = s.activation(out=s_ms[:, 3:4], in_=s_ms[:, 3:4], func=AF.Exp, scale=-0.5)
                    em.inc(inst, "ac", 1)

                @blk.vector
                def _(v):
                    em.wait(v, "ac")
                    em.wait(v, "dv")
                    inst = v.tensor_scalar(out=s_xn[:], in0=s_x[:], scalar1=s_ms[:, 2:3], op0=ALU.add,
                                           scalar2=s_ms[:, 3:4], op1=ALU.mult)
                    rt(v, inst)
                    inst = v.tensor_copy(s_xnb[:], s_xn[:])
                    em.inc(inst, "dv", 1)

                @blk.tensor
                def _(t):
                    em.wait(t, "dv")
                    em.wait(t, "sx")
                    for c8 in range(8):
                        t.transpose(pT[:, c8 * P:(c8 + 1) * P], s_xnb[:, c8 * P:(c8 + 1) * P],
                                    identity=s_idb[:])
                    em.inc(t.drain(), "tn", 1)

                @blk.vector
                def _(v):
                    em.wait(v, "tn")
                    em.wait(v, "dv")
                    inst = v.tensor_copy(s_xf[:], pT[:, 0:1024])
                    em.inc(inst, "sx", 1)

                @blk.gpsimd
                def _(gq, ti=ti):
                    em.wait(gq, "sx")
                    em.wait(gq, "dv")
                    em.inc(gq.dma_start(out=ap["xn1"][ti * P:(ti + 1) * P, :], in_=s_xn[:]), "st", 16)
                    em.inc(gq.dma_start(
                        out=ap["xnf"].rearrange("(c p) t -> p c t", p=P)[:, :, ti * P:(ti + 1) * P],
                        in_=s_xf.ap().rearrange("p (c t) -> p c t", c=8)), "st", 16)

        full_barrier(blk, em)

        if ph <= 2:
            @blk.sync
            def _(sy):
                em.wait(sy, "st")
                em.wait(sy, "gp")
            em.close()
            return nc

        with nc.sbuf_tensor("r_xf", [P, 8 * P], BF16) as s_rxf, \
             nc.sbuf_tensor("r_q", [P, 6 * P], F32) as s_q6, \
             nc.sbuf_tensor("r_qs", [P, P], F32) as s_qsq, \
             nc.sbuf_tensor("r_st", [8, 4 * P], F32) as s_qst, \
             nc.sbuf_tensor("r_qh", [P, 6 * P], BF16) as s_qh, \
             nc.sbuf_tensor("r_t", [P, P], F32) as s_rt, \
             nc.sbuf_tensor("r_t8", [P, 16], F32) as s_t8, \
             nc.sbuf_tensor("r_e", [P, 256], BF16) as s_e, \
             nc.sbuf_tensor("r_m", [P, 256], BF16) as s_m, \
             nc.sbuf_tensor("r_scb", [P, 256], BF16) as s_scb, \
             nc.sbuf_tensor("r_scf", [P, 2 * P], BF16) as s_scf, \
             nc.sbuf_tensor("r_ov", [P, P], BF16) as s_ov, \
             nc.sbuf_tensor("r_o2", [P, P], F32) as s_o2, \
             nc.sbuf_tensor("r_o2b", [P, P], BF16) as s_o2b:
            for ti in range(NT128):
                @blk.sync
                def _(sy, ti=ti):
                    if ti >= 1:
                        em.wait(sy, "st")
                    em.inc(sy.dma_start(
                        out=s_rxf.ap().rearrange("p (c t) -> p c t", c=8),
                        in_=ap["xnf"].rearrange("(c p) t -> p c t", p=P)[:, :, ti * P:(ti + 1) * P]), "ld", 16)

                @blk.tensor
                def _(t):
                    em.wait(t, "ld")
                    em.wait(t, "dv")
                    for oc in range(6):
                        pp, po = (pA, oc) if oc < 4 else (pB, oc - 4)
                        for dc in range(8):
                            t.matmul(pp[:, po * P:(po + 1) * P],
                                     s_wq[:, dc * 768 + oc * P:dc * 768 + (oc + 1) * P],
                                     s_rxf[:, dc * P:(dc + 1) * P],
                                     start=(dc == 0), stop=(dc == 7))
                    em.inc(t.drain(), "tn", 1)

                @blk.vector
                def _(v):
                    em.wait(v, "tn")
                    em.wait(v, "dv")
                    for oc in range(6):
                        pp, po = (pA, oc) if oc < 4 else (pB, oc - 4)
                        inst = v.tensor_scalar(out=s_q6[:, oc * P:(oc + 1) * P],
                                               in0=pp[:, po * P:(po + 1) * P],
                                               scalar1=s_bq6[:, oc:oc + 1], scalar2=None, op0=ALU.add)
                    em.inc(inst, "dv", 1)

                for oc in range(6):
                    @blk.scalar
                    def _(s, oc=oc):
                        em.wait(s, "dv")
                        em.wait(s, "ac")
                        inst = s.activation(out=s_qsq[:], in_=s_q6[:, oc * P:(oc + 1) * P], func=AF.Square)
                        em.inc(inst, "ac", 1)

                    @blk.tensor
                    def _(t, oc=oc):
                        em.wait(t, "ac")
                        em.wait(t, "dv")
                        t.matmul(pC[0:8, 0:P], s_bdg[:], s_q6[:, oc * P:(oc + 1) * P], start=True, stop=True)
                        t.matmul(pC[0:8, 128:256], s_bdg[:], s_qsq[:], start=True, stop=True)
                        em.inc(t.drain(), "tn", 1)

                    @blk.vector
                    def _(v, oc=oc):
                        em.wait(v, "tn")
                        em.wait(v, "dv")
                        inst = v.tensor_scalar(out=s_qst[:, 0:P], in0=pC[0:8, 0:P], scalar1=1.0 / 16,
                                               scalar2=None, op0=ALU.mult)
                        rt(v, inst)
                        inst = v.tensor_scalar(out=s_qst[:, P:2 * P], in0=pC[0:8, 128:256], scalar1=1.0 / 16,
                                               scalar2=None, op0=ALU.mult)
                        rt(v, inst)
                        inst = v.tensor_tensor(out=s_qst[:, 2 * P:3 * P], in0=s_qst[:, 0:P],
                                               in1=s_qst[:, 0:P], op=ALU.mult)
                        rt(v, inst)
                        inst = v.tensor_tensor(out=s_qst[:, P:2 * P], in0=s_qst[:, P:2 * P],
                                               in1=s_qst[:, 2 * P:3 * P], op=ALU.subtract)
                        em.inc(inst, "dv", 1)

                    @blk.scalar
                    def _(s, oc=oc):
                        em.wait(s, "dv")
                        em.wait(s, "ac")
                        s.activation(out=s_qst[:, P:2 * P], in_=s_qst[:, P:2 * P], func=AF.Ln,
                                     bias=s_eps[0:8, 0:1])
                        inst = s.activation(out=s_qst[:, P:2 * P], in_=s_qst[:, P:2 * P], func=AF.Exp,
                                            scale=-0.5)
                        em.inc(inst, "ac", 1)

                    @blk.tensor
                    def _(t, oc=oc):
                        em.wait(t, "ac")
                        em.wait(t, "dv")
                        t.matmul(pD[:, 0:P], s_bdgt[:], s_qst[:, 0:P], start=True, stop=True)
                        t.matmul(pD[:, 128:256], s_bdgt[:], s_qst[:, P:2 * P], start=True, stop=True)
                        em.inc(t.drain(), "tn", 1)

                    @blk.vector
                    def _(v, oc=oc):
                        em.wait(v, "tn")
                        em.wait(v, "dv")
                        inst = v.tensor_tensor(out=s_rt[:], in0=s_q6[:, oc * P:(oc + 1) * P],
                                               in1=pD[:, 0:P], op=ALU.subtract)
                        rt(v, inst)
                        inst = v.tensor_tensor(out=s_rt[:], in0=s_rt[:], in1=pD[:, 128:256], op=ALU.mult)
                        rt(v, inst)
                        inst = v.tensor_scalar(out=s_qh[:, oc * P:(oc + 1) * P], in0=s_rt[:],
                                               scalar1=s_gq[:, 0:1], op0=ALU.mult,
                                               scalar2=s_gq[:, 1:2], op1=ALU.add)
                        em.inc(inst, "dv", 1)

                for g6 in range(6):
                    for kc in range(4):
                        sl0 = g6 * 8 + kc * 2

                        @blk.tensor
                        def _(t, g6=g6, kc=kc):
                            em.wait(t, "dv")
                            em.wait(t, "ac")
                            inst = t.matmul(pE[:, 0:512], s_qh[:, g6 * P:(g6 + 1) * P],
                                            s_bdk[:, g6 * 2048 + kc * 512:g6 * 2048 + (kc + 1) * 512],
                                            start=True, stop=True)
                            em.inc(t.drain(), "tn", 1)

                        for hs in range(2):
                            slot = sl0 + hs

                            @blk.vector
                            def _(v, hs=hs):
                                em.wait(v, "tn")
                                em.wait(v, "dv")
                                inst = v.max(out=s_t8[:, 0:8], in_=pE[:, hs * 256:(hs + 1) * 256])
                                rt(v, inst)
                                inst = v.tensor_scalar(out=s_t8[:, 8:9], in0=s_t8[:, 0:1],
                                                       scalar1=-1.0, scalar2=None, op0=ALU.mult)
                                rt(v, inst)
                                inst = v.tensor_scalar(out=s_m[:, 0:256], in0=pE[:, hs * 256:(hs + 1) * 256],
                                                       scalar1=s_t8[:, 7:8], op0=ALU.is_ge,
                                                       scalar2=None)
                                em.inc(inst, "dv", 1)

                            @blk.scalar
                            def _(s, hs=hs):
                                em.wait(s, "dv")
                                em.wait(s, "ac")
                                inst = s.activation(out=s_t8[:, 0:8], in_=s_t8[:, 0:8], func=AF.Exp,
                                                    bias=s_t8[:, 8:9])
                                em.inc(inst, "ac", 1)

                            @blk.vector
                            def _(v, hs=hs):
                                em.wait(v, "ac")
                                em.wait(v, "dv")
                                inst = v.tensor_reduce(out=s_t8[:, 9:10], in_=s_t8[:, 0:8], axis=AX.X,
                                                       op=ALU.add)
                                rt(v, inst)

                            @blk.scalar
                            def _(s, hs=hs):
                                em.wait(s, "dv")
                                em.wait(s, "ac")
                                inst = s.activation(out=s_t8[:, 11:12], in_=s_t8[:, 9:10], func=AF.Ln)
                                em.inc(inst, "ac", 1)

                            @blk.vector
                            def _(v, hs=hs):
                                em.wait(v, "ac")
                                em.wait(v, "dv")
                                inst = v.tensor_tensor(out=s_t8[:, 12:13], in0=s_t8[:, 8:9],
                                                       in1=s_t8[:, 11:12], op=ALU.subtract)
                                em.inc(inst, "dv", 1)

                            @blk.scalar
                            def _(s, hs=hs):
                                em.wait(s, "dv")
                                em.wait(s, "ac")
                                inst = s.activation(out=s_e[:, 0:256], in_=pE[:, hs * 256:(hs + 1) * 256],
                                                    func=AF.Exp, bias=s_t8[:, 12:13])
                                em.inc(inst, "ac", 1)

                            @blk.vector
                            def _(v, hs=hs):
                                em.wait(v, "ac")
                                em.wait(v, "dv")
                                inst = v.tensor_tensor(out=s_scb[:, 0:256], in0=s_e[:, 0:256],
                                                       in1=s_m[:, 0:256], op=ALU.mult)
                                em.inc(inst, "dv", 1)

                            @blk.tensor
                            def _(t, hs=hs):
                                em.wait(t, "dv")
                                t.transpose(pT[:, 0:P], s_scb[:, 0:P], identity=s_idb[:])
                                t.transpose(pT[:, P:2 * P], s_scb[:, P:2 * P], identity=s_idb[:])
                                em.inc(t.drain(), "tn", 1)

                            @blk.vector
                            def _(v, hs=hs):
                                em.wait(v, "tn")
                                em.wait(v, "dv")
                                inst = v.tensor_copy(s_scf[:], pT[:, 0:2 * P])
                                em.inc(inst, "dv", 1)

                            @blk.tensor
                            def _(t, g6=g6, kc=kc, hs=hs, slot=slot):
                                em.wait(t, "dv")
                                jj = (kc * 2 + hs) % 4
                                hg = (kc * 2 + hs) // 4
                                ph = pF if hg == 0 else pC
                                po = 0 if hg == 0 else 256
                                if jj == 0:
                                    t.matmul(ph[:, po:po + P],
                                             s_bwr[:, (g6 * 2 + hg) * P:(g6 * 2 + hg + 1) * P],
                                             s_qh[:, g6 * P:(g6 + 1) * P], start=True, stop=False)
                                for vk in range(2):
                                    t.matmul(ph[32 * jj:32 * (jj + 1), po:po + P],
                                             s_vals[:, (slot * 2 + vk) * 32:(slot * 2 + vk + 1) * 32],
                                             s_scf[:, vk * P:(vk + 1) * P],
                                             start=False, stop=(jj == 3 and vk == 1),
                                             tile_position=(0, 32 * jj))
                                em.inc(t.drain(), "tn", 1)

                    for hg in range(2):
                        @blk.vector
                        def _(v, g6=g6, hg=hg):
                            em.wait(v, "tn")
                            em.wait(v, "dv")
                            ph = pF if hg == 0 else pC
                            po = 0 if hg == 0 else 256
                            inst = v.tensor_scalar(out=s_ov[:], in0=ph[:, po:po + P],
                                                   scalar1=s_bres[:, 0:1], scalar2=None, op0=ALU.add)
                            em.inc(inst, "dv", 1)

                        @blk.tensor
                        def _(t, g6=g6, hg=hg):
                            em.wait(t, "dv")
                            sg = g6 * 2 + hg
                            for c2 in range(2):
                                t.matmul(pG[:, c2 * P:(c2 + 1) * P],
                                         s_bwm[:, (sg * 2 + c2) * P:(sg * 2 + c2 + 1) * P],
                                         s_ov[:], start=True, stop=True)
                            em.inc(t.drain(), "tn", 1)

                        for c2 in range(2):
                            @blk.vector
                            def _(v, g6=g6, hg=hg, c2=c2):
                                em.wait(v, "tn")
                                em.wait(v, "dv")
                                inst = v.tensor_scalar(out=s_o2[:], in0=pG[:, c2 * P:(c2 + 1) * P],
                                                       scalar1=s_bres[:, 1:2], scalar2=None, op0=ALU.add)
                                rt(v, inst)

                            if True:
                                @blk.vector
                                def _(v, g6=g6):
                                    em.wait(v, "dv")
                                    if g6 in (0, 1, 4, 5):
                                        inst = v.tensor_scalar(out=s_rt[:], in0=s_o2[:], scalar1=0.0,
                                                               scalar2=None, op0=ALU.min)
                                        rt(v, inst)
                                    else:
                                        inst = v.tensor_copy(s_rt[:], s_o2[:])
                                        rt(v, inst)

                                @blk.scalar
                                def _(s, g6=g6):
                                    em.wait(s, "dv")
                                    em.wait(s, "ac")
                                    if g6 in (0, 1, 4, 5):
                                        inst = s.activation(out=s_rt[:], in_=s_rt[:], func=AF.Exp)
                                    else:
                                        inst = s.activation(out=s_rt[:], in_=s_rt[:], func=AF.Identity)
                                    em.inc(inst, "ac", 1)

                                @blk.vector
                                def _(v, g6=g6):
                                    em.wait(v, "ac")
                                    em.wait(v, "dv")
                                    if g6 in (0, 1, 4, 5):
                                        inst = v.tensor_scalar(out=s_o2[:], in0=s_o2[:], scalar1=0.0,
                                                               scalar2=None, op0=ALU.max)
                                        rt(v, inst)
                                        inst = v.tensor_tensor(out=s_o2b[:], in0=s_o2[:], in1=s_rt[:],
                                                               op=ALU.add)
                                        rt(v, inst)
                                    else:
                                        inst = v.tensor_copy(s_o2b[:], s_o2[:])
                                        rt(v, inst)

                            if g6 in (2, 3):
                                @blk.tensor
                                def _(t):
                                    em.wait(t, "dv")
                                    t.transpose(pT[:, 0:P], s_o2b[:], identity=s_idb[:])
                                    em.inc(t.drain(), "tn", 1)

                                @blk.vector
                                def _(v):
                                    em.wait(v, "tn")
                                    em.wait(v, "dv")
                                    inst = v.tensor_copy(s_scf[:, 0:P], pT[:, 0:P])
                                    em.inc(inst, "dv", 1)

                                @blk.gpsimd
                                def _(gq, ti=ti, g6=g6, hg=hg, c2=c2):
                                    em.wait(gq, "dv")
                                    col0 = ((g6 - 2) * 4 + hg * 2 + c2) * P
                                    inst = gq.dma_start(out=ap["hvt"][ti * P:(ti + 1) * P, col0:col0 + P],
                                                        in_=s_scf[:, 0:P])
                                    em.inc(inst, "st", 16)
                            else:
                                @blk.gpsimd
                                def _(gq, ti=ti, g6=g6, hg=hg, c2=c2):
                                    em.wait(gq, "dv")
                                    base = 0 if g6 < 2 else 4
                                    row0 = ((g6 - base) * 4 + hg * 2 + c2) * P
                                    dst = ap["hkf"] if g6 < 2 else ap["hqf"]
                                    inst = gq.dma_start(out=dst.rearrange("(c p) t -> c p t", p=P)[row0 // P, :, ti * P:(ti + 1) * P],
                                                        in_=s_o2b[:])
                                    em.inc(inst, "st", 16)

        full_barrier(blk, em)

        if ph <= 3:
            @blk.sync
            def _(sy):
                em.wait(sy, "st")
                em.wait(sy, "gp")
            em.close()
            return nc

        NLG = ntok // P
        with nc.sbuf_tensor("l_q", [P, ntok], BF16) as s_lq, \
             nc.sbuf_tensor("l_k", [P, ntok], BF16) as s_lk, \
             nc.sbuf_tensor("l_v", [P, 64], BF16) as s_lv, \
             nc.sbuf_tensor("l_at", [P, P], BF16) as s_at, \
             nc.sbuf_tensor("l_lo", [P, 64], F32) as s_lo, \
             nc.sbuf_tensor("l_lob", [64, P], BF16) as s_lob, \
             nc.sbuf_tensor("l_c", [P, 4], F32) as s_lc:
            for h2 in range(8):
                @blk.sync
                def _(sy, h2=h2):
                    if h2 >= 1:
                        em.wait(sy, "st")
                        em.wait(sy, "tn")
                    em.inc(sy.dma_start(out=s_lq[:], in_=ap["hqf"][h2 * P:(h2 + 1) * P, :]), "ld", 16)
                    em.inc(sy.dma_start(out=s_lk[:], in_=ap["hkf"][h2 * P:(h2 + 1) * P, :]), "ld", 16)

                for hh in range(2):
                    h = h2 * 2 + hh
                    for lg in range(NLG):
                        @blk.sync
                        def _(sy, h=h, lg=lg):
                            if lg >= 1 or h > 0:
                                em.wait(sy, "tn")
                            em.inc(sy.dma_start(out=s_lv[:],
                                                in_=ap["hvt"][lg * P:(lg + 1) * P, h * 64:(h + 1) * 64]), "ld", 16)

                        @blk.tensor
                        def _(t, h=h, hh=hh, lg=lg):
                            em.wait(t, "ld")
                            em.wait(t, "dv")
                            inst = t.matmul(pE[:, 0:P], s_lk[hh * 64:(hh + 1) * 64, lg * P:(lg + 1) * P],
                                            s_lq[hh * 64:(hh + 1) * 64, lg * P:(lg + 1) * P],
                                            start=True, stop=True)
                            em.inc(t.drain(), "tn", 1)

                        @blk.vector
                        def _(v):
                            em.wait(v, "tn")
                            em.wait(v, "dv")
                            inst = v.tensor_tensor(out=s_at[:], in0=pE[:, 0:P], in1=s_blk[:], op=ALU.mult)
                            em.inc(inst, "dv", 1)

                        @blk.tensor
                        def _(t):
                            em.wait(t, "dv")
                            t.matmul(pF[:, 0:64], s_at[:], s_lv[:], start=True, stop=False)
                            t.matmul(pF[:, 64:65], s_at[:], s_onesb[:], start=False, stop=True)
                            em.inc(t.drain(), "tn", 1)

                        @blk.vector
                        def _(v):
                            em.wait(v, "tn")
                            em.wait(v, "dv")
                            inst = v.tensor_scalar(out=s_lc[:, 0:1], in0=pF[:, 64:65], scalar1=1e-5,
                                                   scalar2=None, op0=ALU.add)
                            rt(v, inst)
                            inst = v.reciprocal(out=s_lc[:, 1:2], in_=s_lc[:, 0:1])
                            rt(v, inst)
                            inst = v.tensor_scalar(out=s_lo[:], in0=pF[:, 0:64],
                                                   scalar1=s_lc[:, 1:2], op0=ALU.mult,
                                                   scalar2=0.125, op1=ALU.mult)
                            em.inc(inst, "dv", 1)

                        @blk.tensor
                        def _(t):
                            em.wait(t, "dv")
                            t.transpose(pG[0:64, 0:P], s_lo[:], identity=s_id[:])
                            em.inc(t.drain(), "tn", 1)

                        @blk.vector
                        def _(v):
                            em.wait(v, "tn")
                            em.wait(v, "dv")
                            inst = v.tensor_copy(s_lob[:], pG[0:64, 0:P])
                            em.inc(inst, "dv", 1)

                        @blk.gpsimd
                        def _(gq, h=h, lg=lg):
                            em.wait(gq, "dv")
                            inst = gq.dma_start(out=ap["lof"][h * 64:(h + 1) * 64, lg * P:(lg + 1) * P],
                                                in_=s_lob[:])
                            em.inc(inst, "st", 16)

        full_barrier(blk, em)

        if ph <= 4:
            @blk.sync
            def _(sy):
                em.wait(sy, "st")
                em.wait(sy, "gp")
            em.close()
            return nc

        with nc.sbuf_tensor("w_lo", [P, 8 * P], BF16) as s_wlo, \
             nc.sbuf_tensor("w_x", [P, D], F32) as s_wx, \
             nc.sbuf_tensor("w_y", [P, D], F32) as s_wy, \
             nc.sbuf_tensor("w_mv", [P, 12], F32) as w_mv, nc.sbuf_tensor("w_ms", [P, 8], F32) as w_ms:
            for ti in range(NT128):
                @blk.sync
                def _(sy, ti=ti):
                    if ti >= 1:
                        em.wait(sy, "st")
                    em.inc(sy.dma_start(
                        out=s_wlo.ap().rearrange("p (c t) -> p c t", c=8),
                        in_=ap["lof"].rearrange("(c p) t -> p c t", p=P)[:, :, ti * P:(ti + 1) * P]), "ld", 16)
                    em.inc(sy.dma_start(out=s_wx[:], in_=ap["xn1"][ti * P:(ti + 1) * P, :]), "ld", 16)

                @blk.tensor
                def _(t):
                    em.wait(t, "ld")
                    em.wait(t, "dv")
                    for half in range(2):
                        pp = pA if half == 0 else pB
                        for dc in range(8):
                            t.matmul(pp[:, 0:512], s_wlo[:, dc * P:(dc + 1) * P],
                                     s_wo[:, dc * D + half * 512:dc * D + (half + 1) * 512],
                                     start=(dc == 0), stop=(dc == 7))
                    em.inc(t.drain(), "tn", 1)

                @blk.vector
                def _(v):
                    em.wait(v, "tn")
                    em.wait(v, "dv")
                    inst = v.tensor_tensor(out=s_wy[:, 0:512], in0=pA[:, 0:512], in1=s_wx[:, 0:512], op=ALU.add)
                    inst = v.tensor_tensor(out=s_wy[:, 512:1024], in0=pB[:, 0:512],
                                           in1=s_wx[:, 512:1024], op=ALU.add)
                    rt(v, inst)
                    inst = v.bn_stats(w_mv[:, 0:6], s_wy[:, 0:512])
                    inst = v.bn_stats(w_mv[:, 6:12], s_wy[:, 512:1024])
                    rt(v, inst)
                    inst = v.bn_aggr(w_ms[:, 0:2], w_mv[:, 0:12])
                    rt(v, inst)
                    inst = v.tensor_scalar(out=w_ms[:, 2:3], in0=w_ms[:, 0:1], scalar1=-1.0,
                                           scalar2=None, op0=ALU.mult)
                    em.inc(inst, "dv", 1)

                @blk.scalar
                def _(s):
                    em.wait(s, "dv")
                    em.wait(s, "ac")
                    s.activation(out=w_ms[:, 3:4], in_=w_ms[:, 1:2], func=AF.Ln, bias=s_eps[:, 0:1])
                    inst = s.activation(out=w_ms[:, 3:4], in_=w_ms[:, 3:4], func=AF.Exp, scale=-0.5)
                    em.inc(inst, "ac", 1)

                @blk.vector
                def _(v):
                    em.wait(v, "ac")
                    em.wait(v, "dv")
                    inst = v.tensor_scalar(out=s_wy[:], in0=s_wy[:], scalar1=w_ms[:, 2:3], op0=ALU.add,
                                           scalar2=w_ms[:, 3:4], op1=ALU.mult)
                    rt(v, inst)
                    inst = v.tensor_tensor(out=s_wy[:], in0=s_wy[:], in1=s_gout[:], op=ALU.mult)
                    rt(v, inst)
                    inst = v.tensor_tensor(out=s_wy[:], in0=s_wy[:], in1=s_bout[:], op=ALU.add)
                    em.inc(inst, "dv", 1)

                @blk.gpsimd
                def _(gq, ti=ti):
                    em.wait(gq, "dv")
                    em.inc(gq.dma_start(out=ap["atn"][ti * P:(ti + 1) * P, :], in_=s_wy[:]), "st", 16)

        full_barrier(blk, em)

        if ph <= 5:
            @blk.sync
            def _(sy):
                em.wait(sy, "st")
                em.wait(sy, "gp")
            em.close()
            return nc

        with nc.sbuf_tensor("f_o", [P, D], F32) as s_fo, \
             nc.sbuf_tensor("f_ob", [P, D], BF16) as s_fob, \
             nc.sbuf_tensor("f_of", [P, 8 * P], BF16) as s_fof, \
             nc.sbuf_tensor("f_h1", [P, 8 * P], BF16) as s_fh1, \
             nc.sbuf_tensor("f_h2", [P, 4 * P], BF16) as s_fh2, \
             nc.sbuf_tensor("f_h3", [P, 2 * P], BF16) as s_fh3, \
             nc.sbuf_tensor("f_c", [P, 6], F32) as s_fc:
            for cyc in range(3):
                src_t = ap["xn1"] if cyc == 0 else ap["ffs"]
                for ti in range(NT128):
                    @blk.sync
                    def _(sy, ti=ti, src_t=src_t):
                        if ti >= 1 or cyc > 0:
                            em.wait(sy, "st")
                        em.inc(sy.dma_start(out=s_fo[:], in_=src_t[ti * P:(ti + 1) * P, :]), "ld", 16)

                    @blk.vector
                    def _(v):
                        em.wait(v, "ld")
                        em.wait(v, "dv")
                        inst = v.tensor_copy(s_fob[:], s_fo[:])
                        em.inc(inst, "dv", 1)

                    @blk.tensor
                    def _(t):
                        em.wait(t, "dv")
                        em.wait(t, "sx")
                        for c8 in range(8):
                            t.transpose(pT[:, c8 * P:(c8 + 1) * P], s_fob[:, c8 * P:(c8 + 1) * P],
                                        identity=s_idb[:])
                        em.inc(t.drain(), "tn", 1)

                    @blk.vector
                    def _(v):
                        em.wait(v, "tn")
                        em.wait(v, "dv")
                        inst = v.tensor_copy(s_fof[:], pT[:, 0:1024])
                        em.inc(inst, "sx", 1)

                    @blk.tensor
                    def _(t):
                        em.wait(t, "sx")
                        em.wait(t, "ac")
                        ac0 = em.c["ac"]
                        for hc in range(8):
                            if hc >= 2:
                                t.wait_ge(em.s["ac"], ac0 + hc - 1)
                            pp = pA if hc % 2 == 0 else pB
                            for dc in range(8):
                                t.matmul(pp[:, 0:P], s_cn1[:, dc * D + hc * P:dc * D + (hc + 1) * P],
                                         s_fof[:, dc * P:(dc + 1) * P], start=(dc == 0), stop=(dc == 7))
                            em.inc(t.drain(), "tn", 1)

                    @blk.scalar
                    def _(s):
                        tn0 = em.c["tn"] - 8
                        em.wait(s, "ac")
                        for hc in range(8):
                            s.wait_ge(em.s["tn"], tn0 + hc + 1)
                            pp = pA if hc % 2 == 0 else pB
                            inst = s.activation(out=s_fh1[:, hc * P:(hc + 1) * P], in_=pp[:, 0:P],
                                                func=AF.Gelu, bias=s_cb1[:, hc:hc + 1])
                            em.inc(inst, "ac", 1)

                    @blk.tensor
                    def _(t):
                        em.wait(t, "ac")
                        ac0 = em.c["ac"]
                        for hc in range(4):
                            if hc >= 2:
                                t.wait_ge(em.s["ac"], ac0 + hc - 1)
                            pp = pC if hc % 2 == 0 else pD
                            for dc in range(8):
                                t.matmul(pp[:, 0:P], s_cn2[:, dc * 512 + hc * P:dc * 512 + (hc + 1) * P],
                                         s_fh1[:, dc * P:(dc + 1) * P], start=(dc == 0), stop=(dc == 7))
                            em.inc(t.drain(), "tn", 1)

                    @blk.scalar
                    def _(s):
                        tn0 = em.c["tn"] - 4
                        em.wait(s, "ac")
                        for hc in range(4):
                            s.wait_ge(em.s["tn"], tn0 + hc + 1)
                            pp = pC if hc % 2 == 0 else pD
                            inst = s.activation(out=s_fh2[:, hc * P:(hc + 1) * P], in_=pp[:, 0:P],
                                                func=AF.Gelu, bias=s_cb2[:, hc:hc + 1])
                            em.inc(inst, "ac", 1)

                    @blk.tensor
                    def _(t):
                        em.wait(t, "ac")
                        for hc in range(2):
                            pp = pE if hc % 2 == 0 else pF
                            for dc in range(4):
                                t.matmul(pp[:, 0:P], s_cn3[:, dc * 256 + hc * P:dc * 256 + (hc + 1) * P],
                                         s_fh2[:, dc * P:(dc + 1) * P], start=(dc == 0), stop=(dc == 3))
                            em.inc(t.drain(), "tn", 1)

                    @blk.scalar
                    def _(s):
                        tn0 = em.c["tn"] - 2
                        em.wait(s, "ac")
                        for hc in range(2):
                            s.wait_ge(em.s["tn"], tn0 + hc + 1)
                            pp = pE if hc % 2 == 0 else pF
                            inst = s.activation(out=s_fh3[:, hc * P:(hc + 1) * P], in_=pp[:, 0:P],
                                                func=AF.Gelu, bias=s_cb3[:, hc:hc + 1])
                            em.inc(inst, "ac", 1)

                    @blk.tensor
                    def _(t):
                        em.wait(t, "ac")
                        for hc in range(2):
                            t.matmul(pG[:, 0:1], s_fh3[:, hc * P:(hc + 1) * P], s_cn4[:, hc:hc + 1],
                                     start=(hc == 0), stop=(hc == 1))
                        em.inc(t.drain(), "tn", 1)

                    @blk.vector
                    def _(v):
                        em.wait(v, "tn")
                        em.wait(v, "dv")
                        inst = v.tensor_scalar(out=s_fc[:, 0:1], in0=pG[:, 0:1],
                                               scalar1=s_cb4[:, 0:1], scalar2=None, op0=ALU.add)
                        em.inc(inst, "dv", 1)

                    @blk.scalar
                    def _(s):
                        em.wait(s, "dv")
                        em.wait(s, "ac")
                        inst = s.activation(out=s_fc[:, 0:1], in_=s_fc[:, 0:1], func=AF.Sigmoid)
                        em.inc(inst, "ac", 1)

                    @blk.vector
                    def _(v, ti=ti):
                        em.wait(v, "ac")
                        em.wait(v, "dv")
                        inst = v.tensor_scalar(out=s_fc[:, 1:2], in0=s_fc[:, 0:1], scalar1=0.3,
                                               op0=ALU.mult, scalar2=s_gfb[:, 1:2], op1=ALU.add)
                        rt(v, inst)
                        inst = v.tensor_scalar(out=s_fc[:, 1:2], in0=s_fc[:, 1:2], scalar1=-0.5,
                                               op0=ALU.add, scalar2=2.0, op1=ALU.mult)
                        rt(v, inst)
                        inst = v.tensor_scalar(out=s_fc[:, 1:2], in0=s_fc[:, 1:2],
                                               scalar1=s_gfb[:, 2:3], op0=ALU.mult,
                                               scalar2=0.1, op1=ALU.add)
                        rt(v, inst)
                        inst = v.tensor_scalar(out=s_fc[:, 2:3], in0=s_fc[:, 1:2], scalar1=-1.0,
                                               scalar2=None, op0=ALU.mult)
                        rt(v, inst)
                        d4 = D // 4
                        inst = v.scalar_tensor_tensor(out=s_fo[:, 0:d4], in0=s_fo[:, 2 * d4:3 * d4],
                                                      scalar=s_fc[:, 1:2], in1=s_fo[:, 0:d4],
                                                      op0=ALU.mult, op1=ALU.add)
                        inst = v.scalar_tensor_tensor(out=s_fo[:, d4:2 * d4], in0=s_fo[:, 3 * d4:4 * d4],
                                                      scalar=s_fc[:, 2:3], in1=s_fo[:, d4:2 * d4],
                                                      op0=ALU.mult, op1=ALU.add)
                        inst = v.scalar_tensor_tensor(out=s_fo[:, 2 * d4:3 * d4], in0=s_fob[:, 0:d4],
                                                      scalar=s_fc[:, 2:3], in1=s_fo[:, 2 * d4:3 * d4],
                                                      op0=ALU.mult, op1=ALU.add)
                        inst = v.scalar_tensor_tensor(out=s_fo[:, 3 * d4:4 * d4], in0=s_fob[:, d4:2 * d4],
                                                      scalar=s_fc[:, 1:2], in1=s_fo[:, 3 * d4:4 * d4],
                                                      op0=ALU.mult, op1=ALU.add)
                        em.inc(inst, "dv", 1)

                    @blk.gpsimd
                    def _(gq, ti=ti):
                        em.wait(gq, "dv")
                        em.inc(gq.dma_start(out=ap["ffs"][ti * P:(ti + 1) * P, :], in_=s_fo[:]), "st", 16)

                full_barrier(blk, em)

        if ph <= 6:
            @blk.sync
            def _(sy):
                em.wait(sy, "st")
                em.wait(sy, "gp")
            em.close()
            return nc

        with nc.sbuf_tensor("z_a", [P, D], F32) as s_za, nc.sbuf_tensor("z_b", [P, D], F32) as s_zb, \
             nc.sbuf_tensor("z_c", [P, D], F32) as s_zc:
            for ti in range(NT128):
                @blk.sync
                def _(sy, ti=ti):
                    if ti >= 1:
                        em.wait(sy, "st")
                    em.inc(sy.dma_start(out=s_za[:], in_=ap["x1"][ti * P:(ti + 1) * P, :]), "ld", 16)
                    em.inc(sy.dma_start(out=s_zb[:], in_=ap["atn"][ti * P:(ti + 1) * P, :]), "ld", 16)
                    em.inc(sy.dma_start(out=s_zc[:], in_=ap["ffs"][ti * P:(ti + 1) * P, :]), "ld", 16)

                @blk.vector
                def _(v):
                    em.wait(v, "ld")
                    em.wait(v, "dv")
                    inst = v.tensor_tensor(out=s_za[:], in0=s_za[:], in1=s_zb[:], op=ALU.add)
                    rt(v, inst)
                    inst = v.tensor_tensor(out=s_za[:], in0=s_za[:], in1=s_zc[:], op=ALU.add)
                    em.inc(inst, "dv", 1)

                @blk.gpsimd
                def _(gq, ti=ti):
                    em.wait(gq, "dv")
                    em.inc(gq.dma_start(out=ap["x2"][ti * P:(ti + 1) * P, :], in_=s_za[:]), "st", 16)

        @blk.sync
        def _(sy):
            em.wait(sy, "st")

        em.close()
    return nc


def l1_inputs(x1_core, gc_full, d):
    f = np.float32
    g = lambda n: np.asarray(d[n], f)
    Wq = g("L1_attn_Wq")
    wqt = np.zeros((P, 8 * 768), f)
    for dc in range(8):
        wqt[:, dc * 768:(dc + 1) * 768] = Wq[:, dc * P:(dc + 1) * P].T
    keys = g("L1_attn_keys")
    krows = np.concatenate([keys, keys[:16]], axis=0).reshape(-1, 16)
    vals_cat = np.concatenate([g("L1_attn_values"), g("L1_attn_values")[:16]], axis=0)
    vals = np.zeros((P, 48 * 2 * 32), f)
    for s in range(48):
        for vk in range(2):
            vals[:, (s * 2 + vk) * 32:(s * 2 + vk + 1) * 32] = vals_cat[s, vk * P:(vk + 1) * P, :]
    Wres = g("L1_attn_Wres")
    bdwres = np.zeros((P, 12 * P), f)
    for g6 in range(6):
        for hg in range(2):
            blkw = np.zeros((P, P), f)
            for j in range(4):
                sl = g6 * 8 + hg * 4 + j
                lane = sl % 8
                blkw[16 * lane:16 * lane + 16, 32 * j:32 * (j + 1)] = Wres.T
            bdwres[:, (g6 * 2 + hg) * P:(g6 * 2 + hg + 1) * P] = blkw
    Wmem = g("L1_attn_Wmem")
    bdwmem = np.zeros((P, 24 * P), f)
    for sg in range(12):
        for c2 in range(2):
            blkw = np.zeros((P, P), f)
            for jj in range(2):
                j = c2 * 2 + jj
                blkw[32 * j:32 * (j + 1), 64 * jj:64 * (jj + 1)] = Wmem.T
            bdwmem[:, (sg * 2 + c2) * P:(sg * 2 + c2 + 1) * P] = blkw
    Wo1 = g("L1_attn_Wo")
    wot = np.zeros((P, 8 * D), f)
    for dc in range(8):
        wot[:, dc * D:(dc + 1) * D] = Wo1[:, dc * P:(dc + 1) * P].T
    cn1 = np.zeros((P, 8 * D), f)
    w = g("L1_ffn_cn_w1")
    for dc in range(8):
        cn1[:, dc * D:(dc + 1) * D] = w[dc * P:(dc + 1) * P, :]
    cn2 = np.zeros((P, 8 * 512), f)
    w = g("L1_ffn_cn_w2")
    for dc in range(8):
        cn2[:, dc * 512:(dc + 1) * 512] = w[dc * P:(dc + 1) * P, :]
    cn3 = np.zeros((P, 4 * 256), f)
    w = g("L1_ffn_cn_w3")
    for dc in range(4):
        cn3[:, dc * 256:(dc + 1) * 256] = w[dc * P:(dc + 1) * P, :]
    cn4 = np.zeros((P, 2), f)
    w = g("L1_ffn_cn_w4")
    for dc in range(2):
        cn4[:, dc:dc + 1] = w[dc * P:(dc + 1) * P, :]
    gc1 = np.zeros((P, 8 * 512), f)
    w = g("L1_ffn_gc_w1")
    for dc in range(8):
        gc1[:, dc * 512:(dc + 1) * 512] = w[dc * P:(dc + 1) * P, :]
    gc2 = np.zeros((P, 4 * 256), f)
    w = g("L1_ffn_gc_w2")
    for dc in range(4):
        gc2[:, dc * 256:(dc + 1) * 256] = w[dc * P:(dc + 1) * P, :]
    gc3 = np.zeros((P, 2), f)
    w = g("L1_ffn_gc_w3")
    for dc in range(2):
        gc3[:, dc:dc + 1] = w[dc * P:(dc + 1) * P, :]
    blkmask = np.zeros((P, P), f)
    for l in range(4):
        blkmask[l * 32:(l + 1) * 32, l * 32:(l + 1) * 32] = np.triu(np.ones((32, 32), f))
    return {
        "x1": np.ascontiguousarray(x1_core), "gcin": np.ascontiguousarray(gc_full),
        "wqt": wqt, "bq6": g("L1_attn_bq").reshape(6, P).T.copy(),
        "gq_c": np.tile(g("L1_attn_gq"), 8).reshape(P, 1),
        "bqln_c": np.tile(g("L1_attn_bq_ln"), 8).reshape(P, 1),
        "krows": krows, "vals": vals, "bdwres": bdwres, "bdwmem": bdwmem,
        "bres_c": np.tile(g("L1_attn_bres"), 4).reshape(P, 1),
        "bmem_c": np.tile(g("L1_attn_bmem"), 2).reshape(P, 1),
        "wot": wot,
        "gout_b": np.tile(g("L1_attn_g_out").reshape(1, D), (P, 1)),
        "bout_b": np.tile(g("L1_attn_b_out").reshape(1, D), (P, 1)),
        "cn1": cn1, "cn2": cn2, "cn3": cn3, "cn4": cn4,
        "cb1": g("L1_ffn_cn_b1").reshape(8, P).T.copy(),
        "cb2": g("L1_ffn_cn_b2").reshape(4, P).T.copy(),
        "cb3": g("L1_ffn_cn_b3").reshape(2, P).T.copy(),
        "cb4_c": np.full((P, 1), g("L1_ffn_cn_b4")[0], f),
        "gc1": gc1, "gc2": gc2, "gc3": gc3,
        "gb1_b": np.tile(g("L1_ffn_gc_b1").reshape(1, 512), (32, 1)),
        "gb2_b": np.tile(g("L1_ffn_gc_b2").reshape(1, 256), (32, 1)),
        "gb3_b": np.full((32, 1), g("L1_ffn_gc_b3")[0], f),
        "arr": g("L1_ffn_adaptive_range").reshape(1, 1),
        "bdg8": np.repeat(np.eye(8, dtype=f), 16, axis=0),
        "bdg8t": np.repeat(np.eye(8, dtype=f), 16, axis=0).T.copy(),
        "ident": np.eye(P, dtype=f),
        "blkmask": blkmask,
        "ones_c": np.ones((P, 1), f),
    }


def kernel(**inputs):
    x = np.asarray(inputs["L0_attn_x"], np.float32)
    nseq, T = B_FULL // NCORE, T_FULL
    nc0 = build_l0(nseq, T)
    in_maps = []
    for c in range(NCORE):
        xc = x[c * nseq:(c + 1) * nseq].reshape(nseq * T, D)
        in_maps.append(l0_inputs(xc, inputs))
    res = run_bass_kernel_spmd(nc0, in_maps, core_ids=list(range(NCORE)))
    x1 = np.concatenate([np.asarray(r["x1"]).reshape(nseq, T, D) for r in res.results], axis=0)
    gc_full = np.concatenate([np.asarray(r["gc"]) for r in res.results], axis=0)
    npos = T // NCORE
    nc1 = build_l1(npos)
    in_maps1 = []
    for c in range(NCORE):
        sl = np.ascontiguousarray(x1[:, c * npos:(c + 1) * npos, :].transpose(1, 0, 2).reshape(npos * 32, D))
        in_maps1.append(l1_inputs(sl, gc_full, inputs))
    res1 = run_bass_kernel_spmd(nc1, in_maps1, core_ids=list(range(NCORE)))
    x2 = np.concatenate([np.asarray(r["x2"]).reshape(npos, 32, D).transpose(1, 0, 2) for r in res1.results], axis=1)
    return np.ascontiguousarray(x2)
